# version 28
# baseline (speedup 1.0000x reference)
import os
import numpy as np
import ml_dtypes
from contextlib import ExitStack

KDBG = int(os.environ.get("KDBG", "99"))

import concourse.bass as bass
import concourse.mybir as mybir
from concourse.bass_utils import run_bass_kernel_spmd

F32 = mybir.dt.float32
BF16 = mybir.dt.bfloat16
I32 = mybir.dt.int32
ALU = mybir.AluOpType
AF = mybir.ActivationFunctionType
AX = mybir.AxisListType

NCORES = 8
NSEQ = 2
NTT = 16
NCH = 8
SL = 2048
D = 1024
NG = 4

ENGS = ("sync", "gpsimd", "tensor", "vector", "scalar")
PI = float(np.pi)
LN1E4 = float(np.log(10000.0))


class Sched:
    def __init__(self, nc, ctx):
        self.nc = nc
        self.prog = {e: [] for e in ENGS}
        self.sem = {}
        self.cnt = {}
        self.names = ("in", "out", "pre", "wh", "xi", "T", "V", "S", "G")
        self.waited = {(e, s): 0 for e in ENGS for s in self.names}
        for s in self.names:
            self.sem[s] = ctx.enter_context(nc.semaphore("sem_" + s))
            self.cnt[s] = 0

    def emit(self, eng, fn, waits=(), inc=None, inck=1, dep=True):
        wl = []
        for w in waits:
            if w is None:
                continue
            s, v = w
            if v > self.waited[(eng, s)]:
                self.waited[(eng, s)] = v
                wl.append((s, v))
        tokn = None
        if inc is not None:
            self.cnt[inc] += inck
            tokn = (inc, self.cnt[inc])
        self.prog[eng].append((wl, fn, inc if inc is not None else None,
                               inck if inc is not None else 0, dep))
        return tokn

    def barrier(self):
        marks = [(s, self.cnt[s])
                 for s in ("T", "V", "S", "G", "in", "pre", "wh", "xi")
                 if self.cnt[s] > 0]
        for e in ENGS:
            wl = []
            for s, v in marks:
                if v > self.waited[(e, s)]:
                    self.waited[(e, s)] = v
                    wl.append((s, v))
            if wl:
                self.prog[e].append((wl, None, None, 0, True))

    def replay(self, block):
        sem = self.sem

        def mk(eng_name):
            needs_drain = eng_name in ("vector", "scalar")

            def body(engine):
                prev_compute = False
                for wl, fn, inc, inck, dep in self.prog[eng_name]:
                    for s, v in wl:
                        engine.wait_ge(sem[s], v)
                    if fn is not None:
                        if needs_drain and prev_compute and dep:
                            engine.drain()
                        r = fn(engine)
                        prev_compute = True
                        if inc is not None:
                            r.then_inc(sem[inc], inck)
            return body
        block.sync(mk("sync"))
        block.gpsimd(mk("gpsimd"))
        block.tensor(mk("tensor"))
        block.vector(mk("vector"))
        block.scalar(mk("scalar"))


def build_nc():
    nc = bass.Bass("TRN2", target_bir_lowering=False, detect_race_conditions=False)
    ctx = ExitStack()

    x_hbm = nc.dram_tensor("x", [NSEQ, SL, D], F32, kind="ExternalInput")
    out_hbm = nc.dram_tensor("out", [NSEQ, SL, D], F32, kind="ExternalOutput")
    aR_h = {w: nc.dram_tensor(f"aR_{w}", [128, 128], F32, kind="ExternalInput")
            for w in ("q", "k", "v")}
    ab_h = {w: nc.dram_tensor(f"ab_{w}", [1, D], F32, kind="ExternalInput")
            for w in ("q", "k", "v", "p")}
    aPt_h = nc.dram_tensor("aPt", [D, D], F32, kind="ExternalInput")
    bw_h = {w: nc.dram_tensor(f"bw_{w}", [D, D], F32, kind="ExternalInput")
            for w in ("q", "k", "v", "o")}
    fw_h = nc.dram_tensor("fw", [D, D], F32, kind="ExternalInput")
    fb_h = nc.dram_tensor("fb", [1, D], F32, kind="ExternalInput")

    ident_h = nc.inline_tensor(np.eye(128, dtype=ml_dtypes.bfloat16), "identity")
    mnp = np.zeros((128, 128), dtype=ml_dtypes.bfloat16)
    for i in range(128):
        mnp[i, i:] = 1.0
    mask_h = nc.inline_tensor(mnp, "mask01")

    scr = {w: nc.dram_tensor(f"scr_{w}", [D, D], BF16, kind="Internal")
           for w in ("aPt", "bq", "bk", "bv", "bo", "fdqT")}
    dump_h = (nc.dram_tensor("dump", [128, 20480], BF16, kind="ExternalOutput")
              if KDBG < 90 else None)
    dumpf_h = (nc.dram_tensor("dumpf", [128, 2048], F32, kind="ExternalOutput")
               if KDBG < 90 else None)

    sb = lambda n, shp, dt=F32: ctx.enter_context(nc.sbuf_tensor(n, shp, dt))
    x_sb = sb("x_sb", [128, NTT, D])
    ident = sb("ident_sb", [128, 128], BF16)
    mask01 = sb("mask01_sb", [128, 128], BF16)
    consts = sb("consts", [128, 10])
    ones1 = sb("ones1", [65, 128], BF16)
    sums = sb("sums", [128, NTT])
    sumsq = sb("sumsq", [128, NTT])
    lnm = sb("lnm", [128, NTT])
    lnv = sb("lnv", [128, NTT])
    lnr = sb("lnr", [128, NTT])
    negmr = sb("negmr", [128, NTT])
    amax = sb("amax", [128, 4])
    xsc = sb("xsc", [128, 1])
    rs4 = sb("rs4", [128, 4])
    scrA = sb("scrA", [128, D])
    scrB = sb("scrB", [128, 512])
    scrC = sb("scrC", [128, 512])
    scrI = ctx.enter_context(nc.sbuf_tensor("scrI", [128, 512], I32))
    scrI1 = ctx.enter_context(nc.sbuf_tensor("scrI1", [128, 512], I32))
    scrBF = sb("scrBF", [128, 512], BF16)
    _yb0 = sb("yb_t0", [128, D], BF16)
    yb_t2 = [_yb0, _yb0]
    yp_t2 = [sb("yp_t0", [128, NCH, 128], BF16), sb("yp_t1", [128, NCH, 128], BF16)]
    yb_t = yb_t2[0]
    yp_t = yp_t2[0]
    kp = sb("kp", [128, NCH, SL], BF16)
    Vb = sb("Vb", [128, NTT, 1040], BF16)
    qp = sb("qp", [128, NCH, 512], BF16)
    aop = sb("aop", [128, NCH, 512], BF16)
    aotok = [sb(f"aotok{j}", [128, D], BF16) for j in range(4)]
    ptile = [sb(f"pt{j}", [128, 512], BF16) for j in range(2)]
    aRb = {w: sb(f"aRb_{w}", [128, 128], BF16) for w in ("q", "k", "v")}
    wbuf = ctx.enter_context(nc.sbuf_tensor("wbuf", [128, NCH, D], BF16))
    whalf = [ctx.enter_context(nc.sbuf_tensor("whalf0", [128, NCH, 128], BF16)),
             ctx.enter_context(nc.sbuf_tensor("whalf1", [128, NCH, 128], BF16))]
    tabc = sb("tabc", [128, NTT, 32], BF16)
    tabs = sb("tabs", [128, NTT, 64], BF16)
    biasr3 = sb("biasr3", [65, D], BF16)
    biasrp = sb("biasrp", [1, D], BF16)
    biasrow = {"q": 0, "k": 32, "v": 64}
    def bias_ap(w, c0, cn):
        if w == "p":
            return biasrp[0:1, c0:c0 + cn]
        r = biasrow[w]
        return biasr3[r:r + 1, c0:c0 + cn]
    fbB = sb("fbB", [128, D], BF16)

    psc = lambda n, shp, dt=F32: ctx.enter_context(nc.psum_tensor(n, shp, dt))
    psA = psc("psA", [128, D])
    psX = psc("psX", [128, D])
    psY = psc("psY", [128, D])
    psQ = [psA, psX]
    psT2 = [psc("psT0", [128, D], BF16), psc("psT1", [128, D], BF16)]
    psT = psT2[0]

    sch = Sched(nc, ctx)
    E = sch.emit
    tok = {}

    dumped = [False]

    def dump(src_ap, ncols, waits=()):
        if dumped[0] or dump_h is None:
            return None
        dumped[0] = True
        t = E("gpsimd", lambda e: e.dma_start(dump_h.ap()[:, 0:ncols], src_ap),
              waits=waits, inc="out", inck=16)
        return t

    def dumpf(src_ap, col0, ncols, waits=()):
        if dumpf_h is None:
            return None
        t = E("gpsimd", lambda e: e.dma_start(
            dumpf_h.ap()[:, col0:col0 + ncols], src_ap),
            waits=waits, inc="out", inck=16)
        return t

    def dma_in(dst_ap, src_ap, waits=()):
        return E("sync", lambda e, s=src_ap, d=dst_ap: e.dma_start(d, s),
                 waits=waits, inc="in", inck=16)

    def dma_pre(dst_ap, src_ap, waits=()):
        return E("gpsimd", lambda e, s=src_ap, d=dst_ap: e.dma_start(d, s),
                 waits=waits, inc="pre", inck=16)

    _early_tbl = True
    t_ident = dma_in(ident[:], ident_h.ap())
    t_mask = dma_in(mask01[:], mask_h.ap())
    E("vector", lambda e: e.memset(consts[:, 0:1], 0.125), inc="V")
    E("vector", lambda e: e.memset(consts[:, 1:2], 1.0 / 128.0), inc="V")
    E("vector", lambda e: e.memset(consts[:, 2:3], 1e-5), inc="V")
    t_consts = E("vector", lambda e: e.memset(consts[:, 3:4], -0.5), inc="V")
    t_ones = E("vector", lambda e: e.memset(ones1[:], 1.0), inc="V")
    def ones_ap(w):
        r = 0 if w == "p" else biasrow[w]
        return ones1[r:r + 1, :]

    load_table0 = None
    for w in ("q", "k", "v"):
        t = dma_in(scrA[:, 0:128], aR_h[w].ap(), waits=(tok.get("scrA"),))
        tv = E("vector", lambda e, w=w: e.tensor_copy(aRb[w][:], scrA[:, 0:128]),
               waits=(t,), inc="V")
        tok["scrA"] = tv
    for w in ("q", "k", "v", "p"):
        t = dma_in(scrA[0:1, :], ab_h[w].ap(), waits=(tok.get("scrA"),))
        tv = E("vector", lambda e, w=w: e.tensor_copy(
            bias_ap(w, 0, D), scrA[0:1, :]),
               waits=(t,), inc="V")
        tok["scrA"] = tv
    t = dma_in(scrA[:], bass.AP(fb_h, 0, [[0, 128], [1, D]]),
               waits=(tok.get("scrA"),))
    tg = E("vector", lambda e: e.tensor_copy(fbB[:], scrA[:]),
           waits=(t,), inc="V")
    tok["scrA"] = tg


    t_fw = dma_in(x_sb[:, 8:16, :],
                  fw_h.rearrange("(c p) e -> p c e", p=128))
    for ch in range(2):
        t = E("sync", lambda e, ch=ch: e.dma_start(
            x_sb[:, 4 * ch:4 * ch + 4, :],
            x_hbm[0].rearrange("(tt p) c -> p tt c",
                               p=128)[:, 4 * ch:4 * ch + 4, :]),
            waits=(tok.get("xi_prev"),), inc="xi", inck=16)
        tok["xi_prev"] = t
        tok[("xch", 0, ch)] = t
    for c in range(NCH):
        t1 = E("vector", lambda e, c=c: e.tensor_reduce(
            amax[:, 0:1], x_sb[:, 8 + c, :], AX.X, ALU.max,
            apply_absolute_value=True),
            waits=(t_fw,), inc="V")
        E("vector", lambda e: e.tensor_scalar_max(amax[:, 0:1], amax[:, 0:1], 1e-8),
          inc="V")
        t5a = E("vector", lambda e: e.tensor_scalar(
            xsc[:, 0:1], amax[:, 0:1], 1.0 / 127.0, None, ALU.mult), inc="V")
        E("vector", lambda e: e.reciprocal(amax[:, 1:2], xsc[:, 0:1]), inc="V")
        t5 = E("vector", lambda e: e.tensor_scalar(
            rs4[:, 0:1], amax[:, 1:2], -1.0, None, ALU.mult), inc="V")
        for hf in range(2):
            sl_ = slice(512 * hf, 512 * hf + 512)
            ybs = yb_t[:, sl_]
            V = lambda fn, w=(): E("vector", fn, waits=w, inc="V")
            V(lambda e, sl_=sl_, ybs=ybs, c=c: e.tensor_scalar(
                scrC[:], x_sb[:, 8 + c, sl_], amax[:, 1:2], 1.0, ALU.mult,
                ALU.mult))
            V(lambda e, sl_=sl_, ybs=ybs, c=c: e.scalar_tensor_tensor(
                scrD[:], scrC[:], xsc[:, 0:1], x_sb[:, 8 + c, sl_], ALU.mult,
                ALU.subtract))
            V(lambda e, sl_=sl_, ybs=ybs: e.scalar_tensor_tensor(
                scrC[:], scrD[:], rs4[:, 0:1], scrC[:], ALU.mult, ALU.add))
            V(lambda e, sl_=sl_, ybs=ybs: e.tensor_scalar(scrI[:], scrC[:], 1.0, None, ALU.mult))
            V(lambda e, sl_=sl_, ybs=ybs: e.scalar_tensor_tensor(
                scrB[:], scrC[:], 1.0, scrI[:], ALU.mult, ALU.subtract))
            V(lambda e, sl_=sl_, ybs=ybs: e.tensor_scalar(scrD[:], scrI[:], 1.0, None, ALU.mult))
            V(lambda e, sl_=sl_, ybs=ybs: e.tensor_scalar(scrC[:], scrD[:], 0.5, -0.25,
                                        ALU.mult, ALU.add))
            V(lambda e, sl_=sl_, ybs=ybs: e.tensor_scalar(scrI[:], scrC[:], 1.0, None, ALU.mult))
            V(lambda e, sl_=sl_, ybs=ybs: e.scalar_tensor_tensor(scrC[:], scrC[:], 1.0, scrI[:],
                                               ALU.mult, ALU.subtract))
            V(lambda e, sl_=sl_, ybs=ybs: e.tensor_scalar(scrBF[:], scrC[:], 0.5, 1.0,
                                        ALU.is_ge, ALU.mult))
            V(lambda e, sl_=sl_, ybs=ybs: e.tensor_scalar(scrC[:], scrC[:], -0.5, 1.0,
                                        ALU.is_le, ALU.mult))
            V(lambda e, sl_=sl_, ybs=ybs: e.scalar_tensor_tensor(scrC[:], scrC[:], -1.0, scrBF[:],
                                               ALU.mult, ALU.add))
            V(lambda e, sl_=sl_, ybs=ybs: e.scalar_tensor_tensor(scrC[:], scrC[:], 1.0, scrI[:],
                                               ALU.mult, ALU.add))
            V(lambda e, sl_=sl_, ybs=ybs: e.scalar_tensor_tensor(scrC[:], scrC[:], -2.0, scrD[:],
                                               ALU.mult, ALU.add))
            V(lambda e, sl_=sl_, ybs=ybs: e.tensor_scalar(scrBF[:], scrB[:], 0.5, 1.0,
                                        ALU.is_gt, ALU.mult))
            V(lambda e, sl_=sl_, ybs=ybs: e.tensor_scalar(ybs, scrB[:], -0.5, 1.0,
                                        ALU.is_lt, ALU.mult),
              w=(tok.get("yb_t"),))
            V(lambda e, sl_=sl_, ybs=ybs: e.scalar_tensor_tensor(scrBF[:], ybs, -1.0, scrBF[:],
                                               ALU.mult, ALU.add))
            V(lambda e, sl_=sl_, ybs=ybs: e.tensor_scalar(ybs, scrB[:], 0.5, 1.0,
                                        ALU.is_equal, ALU.mult))
            V(lambda e, sl_=sl_, ybs=ybs: e.tensor_scalar(scrB[:], scrB[:], -0.5, 1.0,
                                        ALU.is_equal, ALU.mult))
            V(lambda e, sl_=sl_, ybs=ybs: e.scalar_tensor_tensor(scrB[:], scrB[:], -1.0, ybs,
                                               ALU.mult, ALU.add))
            V(lambda e, sl_=sl_, ybs=ybs: e.tensor_mul(scrB[:], scrB[:], scrC[:]))
            V(lambda e, sl_=sl_, ybs=ybs: e.scalar_tensor_tensor(scrB[:], scrB[:], 1.0, scrBF[:],
                                               ALU.mult, ALU.add))
            V(lambda e, sl_=sl_, ybs=ybs: e.scalar_tensor_tensor(scrB[:], scrB[:], 1.0, scrD[:],
                                               ALU.mult, ALU.add))
            tq = E("vector", lambda e, ybs=ybs: e.tensor_scalar(
                ybs, scrB[:], xsc[:, 0:1], 1.0, ALU.mult, ALU.mult),
                waits=(tok.get("yb_t"),), inc="V")
        tok["scrA"] = tq
        ttr = E("tensor", lambda e: [
            e.transpose(psT[:, 128 * b:128 * b + 128],
                        yb_t[:, 128 * b:128 * b + 128], ident[:])
            for b in range(NCH)][-1],
            waits=(tq, t_ident, tok.get("psT")), inc="T")
        te = E("scalar", lambda e: e.copy(scrA[:].bitcast(BF16)[:, 0:1024], psT[:])
               if False else e.copy(yp_t[:].rearrange("p a b -> p (a b)"), psT[:]),
               waits=(ttr, tok.get("yp_t")), inc="S")
        tok["psT"] = te
        tpl = None
        for b in range(NCH):
            tpl = dma_pre(scr["fdqT"][128 * b:128 * b + 128, 128 * c:128 * c + 128],
                          yp_t[:, b, :], waits=(te,))
        tok["yp_t"] = tpl
        tok["yb_t"] = ttr
        tok["scr_fdqT"] = tpl

    sch.barrier()

    def load_w(name, waits=()):
        src = scr[name].rearrange("(c p) e -> p c e", p=128)
        return E("sync", lambda e, s=src: e.dma_start(wbuf[:], s),
                 waits=tuple(w for w in waits) + (tok.get("scr_" + name),
                                                  tok.get("wbuf_r"),
                                                  tok.get("wbuf_pre")),
                 inc="in", inck=16)

    def x_in(seq, waits=(), chunks=(0, 1, 2, 3)):
        src = x_hbm[seq].rearrange("(tt p) c -> p tt c", p=128)
        ts = []
        for ch in range(4):
            if ch not in chunks:
                ts.append(tok[("xch", seq, ch)])
                continue
            t = E("sync", lambda e, s=src, ch=ch: e.dma_start(
                x_sb[:, 4 * ch:4 * ch + 4, :], s[:, 4 * ch:4 * ch + 4, :]),
                waits=tuple(waits) + (tok.get("xi_prev"),), inc="xi", inck=16)
            tok["xi_prev"] = t
            tok[("xch", seq, ch)] = t
            ts.append(t)
        return ts

    def stats_passes4(g, t_tok, junkB, junkC):
        last = None
        for tt in range(4 * g, 4 * g + 4):
            last = E("vector", lambda e, tt=tt: e.tensor_scalar(
                x_sb[:, tt, :], x_sb[:, tt, :], 1.0, 0.0, ALU.mult, ALU.add,
                accum_out=sums[:, tt:tt + 1]),
                waits=(t_tok,), inc="V", dep=False)
            last = E("vector", lambda e, tt=tt: e.scalar_tensor_tensor(
                junkB, x_sb[:, tt, 0:512], 1.0, x_sb[:, tt, 0:512], ALU.mult,
                ALU.mult, accum_out=sumsq[:, tt:tt + 1]),
                inc="V", dep=False)
            last = E("vector", lambda e, tt=tt: e.scalar_tensor_tensor(
                junkC, x_sb[:, tt, 512:1024], 1.0, x_sb[:, tt, 512:1024],
                ALU.mult, ALU.mult, accum_out=lnv[:, tt:tt + 1]),
                inc="V", dep=False)
        return last

    def ln_stats(t_x, pre=None):
        txl = t_x if isinstance(t_x, list) else [t_x] * 4
        if pre is not None:
            lv = ls = pre
        else:
            lv = ls = None
            for tt in range(NTT):
                tx = txl[tt // 4]
                lv = E("vector", lambda e, tt=tt: e.tensor_scalar(
                    x_sb[:, tt, :], x_sb[:, tt, :], 1.0, 0.0, ALU.mult,
                    ALU.add, accum_out=sums[:, tt:tt + 1]),
                    waits=(tx,), inc="V", dep=(tt == 0))
                ls = E("vector", lambda e, tt=tt: e.scalar_tensor_tensor(
                    psY[:, 0:512], x_sb[:, tt, 0:512], 1.0,
                    x_sb[:, tt, 0:512], ALU.mult, ALU.mult,
                    accum_out=sumsq[:, tt:tt + 1]),
                    inc="V", dep=False)
                ls = E("vector", lambda e, tt=tt: e.scalar_tensor_tensor(
                    psY[:, 512:1024], x_sb[:, tt, 512:1024], 1.0,
                    x_sb[:, tt, 512:1024], ALU.mult, ALU.mult,
                    accum_out=lnv[:, tt:tt + 1]),
                    inc="V", dep=False)
        t_dbg = None
        if KDBG == 31 and not dumped[0]:
            dumped[0] = True
            dumpf(sumsq[:], 0, NTT, waits=(lv, ls))
            dumpf(lnv[:], 16, NTT, waits=(lv, ls))
            t_dbg = dumpf(sums[:], 32, NTT, waits=(lv, ls))
        E("vector", lambda e: e.tensor_scalar(
            lnm[:], sums[:], 1.0 / D, None, ALU.mult),
            waits=(lv, ls, t_dbg), inc="V", dep=False)
        ta1 = E("vector", lambda e: e.tensor_add(lnv[:], lnv[:], sumsq[:]),
                inc="V", dep=False)
        E("vector", lambda e: e.scalar_tensor_tensor(
            sumsq[:], lnm[:], -1.0, lnm[:], ALU.mult, ALU.mult), inc="V",
            dep=False)
        E("vector", lambda e: e.tensor_scalar(
            lnv[:], lnv[:], 1.0 / D, None, ALU.mult),
            waits=(ta1,), inc="V", dep=False)
        t1 = E("vector", lambda e: e.tensor_add(lnv[:], lnv[:], sumsq[:]),
               inc="V")
        t2 = E("scalar", lambda e: e.activation(
            lnr[:], lnv[:], AF.Ln, bias=consts[:, 2:3]),
            waits=(t1, t_consts), inc="S")
        t3e = E("scalar", lambda e: e.activation(
            lnr[:], lnr[:], AF.Exp, scale=consts[:, 3:4]), inc="S")
        t4 = E("vector", lambda e: e.scalar_tensor_tensor(
            negmr[:], lnm[:], -1.0, lnr[:], ALU.mult, ALU.mult),
            waits=(t3e,), inc="V")
        if KDBG == 28 and not dumped[0]:
            dumped[0] = True
            dumpf(sums[:], 0, NTT, waits=(t4,))
            dumpf(lnm[:], 16, NTT, waits=(t4,))
            dumpf(lnv[:], 32, NTT, waits=(t4,))
            dumpf(lnr[:], 48, NTT, waits=(t4,))
            dumpf(sumsq[:], 64, NTT, waits=(t4,))
        return t4

    def apply_ln(tt, dst_ap, t_stats, extra=()):
        return E("scalar", lambda e: e.activation(
            dst_ap, x_sb[:, tt, :], AF.Identity, scale=lnr[:, tt:tt + 1],
            bias=negmr[:, tt:tt + 1]), waits=(t_stats,) + tuple(extra),
            inc="S", dep=(tt == 0))

    def transpose8(src_tile, waits=(), par=0):
        ps = psT2[par]
        def fn(e, ps=ps):
            r = None
            for b in range(NCH):
                r = e.transpose(ps[:, 128 * b:128 * b + 128],
                                src_tile[:, 128 * b:128 * b + 128], ident[:])
            return r
        return E("tensor", fn,
                 waits=tuple(waits) + (tok.get(("psT", par)), t_ident), inc="T")

    def evac_psT(dst_ap, waits=(), par=0):
        ps = psT2[par]
        t = E("scalar", lambda e, ps=ps: e.copy(dst_ap, ps[:]),
              waits=waits, inc="S", dep=False)
        tok[("psT", par)] = t
        return t

    def rope_evac(tt, t_mm, dst=None, ps=None):
        yb_t = dst
        fs = ps.shape[1]
        rot = bass.AP(ps, 0, [[fs, 128], [128, 8], [1, 64]])
        swp = bass.AP(ps, 32, [[fs, 128], [128, 8], [-32, 2], [1, 32]])
        skp = bass.AP(ps, 64, [[fs, 128], [128, 8], [1, 64]])
        c2 = bass.AP(tabc, 32 * tt, [[NTT * 32, 128], [0, 8], [0, 2], [1, 32]])
        s2 = bass.AP(tabs, 64 * tt, [[NTT * 64, 128], [0, 8], [1, 64]])
        ds_ = yb_t.shape[1]
        d_rot = bass.AP(yb_t, 0, [[ds_, 128], [128, 8], [1, 64]])
        d_skp = bass.AP(yb_t, 64, [[ds_, 128], [128, 8], [1, 64]])
        sA = scrB[:].rearrange("p (a b) -> p a b", a=8)
        sB = scrC[:].rearrange("p (a b) -> p a b", a=8)
        E("vector", lambda e: e.tensor_tensor(sA, swp, s2, ALU.mult),
          waits=(t_mm, tok.get("tabs_ok"), tok.get("scrB")),
          inc="V", dep=False)
        E("vector", lambda e: e.tensor_tensor(sB, rot, c2, ALU.mult),
          waits=(tok.get("scrC"),), inc="V", dep=False)
        E("vector", lambda e: e.tensor_copy(d_skp, skp), inc="V", dep=False)
        t3 = E("vector", lambda e: e.tensor_tensor(d_rot, sB, sA, ALU.add),
               inc="V", dep=False)
        tok["scrB"] = t3
        tok["scrC"] = t3
        return t3

    def gen_tables():
        tc0 = E("vector", lambda e: e.memset(consts[:, 9:10], 10000.0), inc="V")
        tc1 = E("scalar", lambda e: e.activation(consts[:, 8:9], consts[:, 9:10],
                                                 AF.Ln), waits=(tc0,), inc="S")
        tc2 = E("vector", lambda e: e.tensor_scalar(
            consts[:, 8:9], consts[:, 8:9], -1.0 / 32.0, None, ALU.mult),
            waits=(tc1,), inc="V")
        tc3 = E("vector", lambda e: e.memset(consts[:, 9:10], 1.0),
                waits=(tc1,), inc="V")
        load_table(_set_trig)
        tc4 = E("scalar", lambda e: e.activation(consts[:, 4:5], consts[:, 9:10],
                                                 AF.Arctan), waits=(tc3,), inc="S")
        tc5 = E("vector", lambda e: e.tensor_scalar(
            consts[:, 4:5], consts[:, 4:5], 4.0, None, ALU.mult),
            waits=(tc4,), inc="V")
        E("vector", lambda e: e.tensor_scalar(
            consts[:, 5:6], consts[:, 4:5], -1.0, None, ALU.mult), inc="V")
        E("vector", lambda e: e.reciprocal(consts[:, 6:7], consts[:, 4:5]), inc="V")
        tc6 = E("vector", lambda e: e.tensor_scalar(
            consts[:, 7:8], consts[:, 4:5], 0.5, None, ALU.mult), inc="V")
        tg1 = E("gpsimd", lambda e: e.iota(scrB[:, 0:NTT], [[128, NTT]], base=0,
                                           channel_multiplier=1,
                                           allow_small_or_imprecise_dtypes=True),
                waits=(tok.get("scrB"),), inc="G")
        tg2 = E("gpsimd", lambda e: e.iota(scrB[:, 16:48], [[1, 32]], base=0,
                                           channel_multiplier=0,
                                           allow_small_or_imprecise_dtypes=True),
                inc="G")
        t2 = E("scalar", lambda e: e.activation(scrB[:, 48:80], scrB[:, 16:48],
                                                AF.Exp, scale=consts[:, 8:9]),
               waits=(tg2, tc2, tc6), inc="S")
        fsB = scrB.shape[1]

        def mk_rad(e):
            src_pos = bass.AP(scrB, 0, [[fsB, 128], [1, NTT], [0, 64]])
            src_ang = bass.AP(scrB, 48, [[fsB, 128], [0, NTT], [0, 2], [1, 32]])
            return e.tensor_tensor(
                psY[:].rearrange("p (a b) -> p a b", a=NTT), src_pos, src_ang,
                ALU.mult)
        t3 = E("vector", mk_rad, waits=(t2, tg1), inc="V")
        if KDBG == 49 and not dumped[0]:
            dumped[0] = True
            t3 = dumpf(psY[:], 0, D, waits=(t3,)) or t3
        if KDBG == 50 and not dumped[0]:
            dumped[0] = True
            t3 = dumpf(scrB[:, 0:80], 0, 80, waits=(t3,)) or t3

        def pipeline(dst_tab, shift, tprev):
            last = tprev
            if shift:
                srcs = [(bass.AP(psY, 0, [[D, 128], [64, NTT], [1, 32]]),
                         dst_tab[:].rearrange("p a b -> p (a b)"))]
            else:
                full = dst_tab[:].rearrange("p a b -> p (a b)")
                srcs = [(psY[:, 0:512], full[:, 0:512]),
                        (psY[:, 512:1024], full[:, 512:1024])]
            for src_ap, dst_ap in srcs:
                t0 = E("vector", lambda e, src_ap=src_ap, shift=shift:
                       e.tensor_scalar(
                           scrC[:].rearrange("p (a b) -> p a b", a=NTT)
                           if shift else scrC[:],
                           src_ap, 1.0,
                           consts[:, 7:8] if shift else 0.0, ALU.mult, ALU.add),
                    waits=(last, tok.get("scrC")), inc="V")
                E("vector", lambda e: e.tensor_scalar(
                    scrI[:], scrC[:], consts[:, 6:7], 1.0, ALU.mult, ALU.mult),
                    inc="V")
                E("vector", lambda e: e.scalar_tensor_tensor(
                    scrB[:], scrC[:], consts[:, 6:7], scrI[:], ALU.mult,
                    ALU.subtract), waits=(tok.get("scrB"),), inc="V")
                E("vector", lambda e: e.tensor_scalar(
                    scrBF[:], scrB[:], 0.5, 1.0, ALU.is_ge, ALU.mult), inc="V")
                E("vector", lambda e: e.tensor_scalar(
                    scrB[:], scrB[:], -0.5, 1.0, ALU.is_le, ALU.mult), inc="V")
                E("vector", lambda e: e.scalar_tensor_tensor(
                    scrB[:], scrB[:], -1.0, scrBF[:], ALU.mult, ALU.add), inc="V")
                E("vector", lambda e: e.scalar_tensor_tensor(
                    scrB[:], scrB[:], 1.0, scrI[:], ALU.mult, ALU.add), inc="V")
                t4 = E("vector", lambda e: e.scalar_tensor_tensor(
                    scrC[:], scrB[:], consts[:, 5:6], scrC[:], ALU.mult,
                    ALU.add), inc="V")
                E("vector", lambda e: e.tensor_scalar(
                    scrI[:], scrB[:], 0.5, -0.25, ALU.mult, ALU.add), inc="V")
                E("vector", lambda e: e.tensor_scalar(
                    scrD[:], scrB[:], 0.5, -0.25, ALU.mult, ALU.add), inc="V")
                E("vector", lambda e: e.scalar_tensor_tensor(
                    scrD[:], scrD[:], 1.0, scrI[:], ALU.mult, ALU.subtract), inc="V")
                E("vector", lambda e: e.tensor_scalar(
                    scrBF[:], scrD[:], 0.5, 1.0, ALU.is_ge, ALU.mult), inc="V")
                E("vector", lambda e: e.tensor_scalar(
                    scrD[:], scrD[:], -0.5, 1.0, ALU.is_le, ALU.mult), inc="V")
                E("vector", lambda e: e.scalar_tensor_tensor(
                    scrD[:], scrD[:], -1.0, scrBF[:], ALU.mult, ALU.add), inc="V")
                E("vector", lambda e: e.scalar_tensor_tensor(
                    scrD[:], scrD[:], 1.0, scrI[:], ALU.mult, ALU.add), inc="V")
                E("vector", lambda e: e.scalar_tensor_tensor(
                    scrB[:], scrD[:], -2.0, scrB[:], ALU.mult, ALU.add), inc="V")
                t5 = E("vector", lambda e: e.tensor_scalar(
                    scrB[:], scrB[:], -2.0, 1.0, ALU.mult, ALU.add), inc="V")
                if KDBG == 51 and hf == 0 and not dumped[0]:
                    dumped[0] = True
                    dumpf(scrC[:], 0, 512, waits=(t4,))
                    t5 = dumpf(scrB[:], 512, 512, waits=(t5,)) or t5
                t6 = E("scalar", lambda e: e.activation(scrD[:], scrC[:], AF.Sin),
                       waits=(t4, t5), inc="S")
                if KDBG == 52 and hf == 0 and not dumped[0]:
                    dumped[0] = True
                    t6 = dumpf(scrD[:], 0, 512, waits=(t6,)) or t6
                    t5 = dumpf(scrB[:], 512, 512, waits=(t5,)) or t5
                last = E("vector", lambda e, dst_ap=dst_ap: e.tensor_tensor(
                    dst_ap, scrB[:], scrD[:], ALU.mult),
                    waits=(t5, t6), inc="V")
                if KDBG == 54 and hf == 0 and not dumped[0]:
                    dumped[0] = True
                    last = E("vector", lambda e: e.tensor_copy(
                        scrA[:, 0:512],
                        dst_tab[:].rearrange("p a b -> p (a b)")[:, 0:512]),
                        waits=(last,), inc="V")
                    last = dumpf(scrA[:, 0:512], 0, 512, waits=(last,)) or last
            tok["scrB"] = last
            tok["scrC"] = last
            return last

        tc = pipeline(tabc, True, t3)
        if KDBG == 53 and not dumped[0]:
            dumped[0] = True
            dump(tabc[:].rearrange("p a b -> p (a b)"), NTT * 64, waits=(tc,))
        tsn = pipeline(tabs, False, tc)
        tf = E("vector", lambda e: e.tensor_scalar(
            bass.AP(tabs, 0, [[NTT * 64, 128], [64, NTT], [1, 32]]),
            bass.AP(tabs, 0, [[NTT * 64, 128], [64, NTT], [1, 32]]),
            -1.0, None, ALU.mult), waits=(tsn,), inc="V")
        load_table(_set_lnexp)
        tok["tabs_ok"] = tf
        return tf

    scrD = sb("scrD", [128, 512])

    from concourse.hw_specs import get_activation_tables
    _tables = list(get_activation_tables(nc.m.arch).values())
    _set_lnexp = next(i for i, fs in enumerate(_tables)
                      if AF.Exp in fs and AF.Ln in fs)
    _set_trig = next(i for i, fs in enumerate(_tables) if AF.Sin in fs)

    def load_table(set_id):
        def fn(e):
            return e.add_instruction(mybir.InstLoadActFuncSet(
                name=nc.get_next_instruction_name(), act_func_set_id=set_id,
                ins=[], outs=[]))
        E("scalar", fn, inc=None)

    load_table(_set_lnexp)

    def attn(kind, t_x, pre=None, tail=None):
        t_xl = t_x if isinstance(t_x, list) else [t_x] * 4
        t_x = t_xl[-1]
        is_a = kind == "a"
        nheads = 16 if is_a else 8
        hd = 64 if is_a else 128
        vstride = 65 if is_a else 130
        sumcol = hd
        escale = consts[:, 0:1] if is_a else consts[:, 1:2]
        kdb = 20 if is_a else 40
        pname = "aPt" if is_a else "bo"

        if KDBG == 59 and not is_a and not dumped[0]:
            dumped[0] = True
            _td = dumpf(x_sb[:, 0, 0:512], 0, 512, waits=(t_x,))
            _td = dumpf(x_sb[:, 0, 512:1024], 512, 512, waits=(t_x,))
        if not is_a:
            if KDBG == 46 and not dumped[0]:
                dumped[0] = True
                dump(tabc[:].rearrange("p a b -> p (a b)"), NTT * 64,
                     waits=(tok.get("tabs_ok"),))
            if KDBG == 48 and not dumped[0]:
                dumped[0] = True
                dump(tabs[:].rearrange("p a b -> p (a b)"), NTT * 64,
                     waits=(tok.get("tabs_ok"),))

        t_stats = ln_stats(t_xl, pre=pre)
        if KDBG == 60 and not is_a and not dumped[0]:
            dumped[0] = True
            dumpf(lnr[:], 0, NTT, waits=(t_stats,))
            dumpf(negmr[:], 16, NTT, waits=(t_stats,))
            dumpf(lnm[:], 32, NTT, waits=(t_stats,))
            dumpf(lnv[:], 48, NTT, waits=(t_stats,))
            _td = dumpf(x_sb[:, 0, 0:512], 512, 512, waits=(t_stats,))
            tok["x_dump"] = _td

        t_ones_v = E("vector", lambda e: e.memset(
            bass.AP(Vb, sumcol, [[NTT * 1040, 128], [1040, NTT],
                                 [vstride, nheads]]), 1.0),
            waits=(tok.get("Vb_r"),), inc="V", dep=False)

        def qkv_mm_a(which, yp, ps):
            def fn(e):
                r = None
                for c in range(NCH):
                    r = e.matmul(ps[:, 128 * c:128 * c + 128],
                                 yp[:, c, :], aRb[which][:],
                                 start=c in (0, 4), stop=False)
                for h2 in range(2):
                    r = e.matmul(ps[:, 512 * h2:512 * h2 + 512],
                                 ones_ap(which),
                                 bias_ap(which, 512 * h2, 512),
                                 start=False, stop=True)
                return r
            return fn

        def qkv_mm_b(yp, ps):
            def fn(e):
                r = None
                for h2 in range(2):
                    for c in range(NCH):
                        r = e.matmul(ps[:, 512 * h2:512 * h2 + 512], yp[:, c, :],
                                     wbuf[:, c, 512 * h2:512 * h2 + 512],
                                     start=(c == 0), stop=(c == NCH - 1))
                return r
            return fn

        def make_tile(tt, which, t_w):
            par = tt % 2
            yb = yb_t2[par]
            yp = yp_t2[par]
            ps = psQ[par]
            t_y = apply_ln(tt, yb[:], t_stats, extra=(tok.get("ybk"),))
            t_tp = transpose8(yb[:], waits=(t_y,), par=par)
            tok["ybk"] = t_tp
            t_yp = evac_psT(yp[:].rearrange("p a b -> p (a b)"),
                            waits=(t_tp, tok.get(("yp", par))), par=par)
            mmfn = (qkv_mm_a(which, yp, ps) if is_a
                    else qkv_mm_b(yp, ps))
            mm_w = (t_yp, t_w, t_ones, tok.get(("psA", par)))
            if par == 1:
                mm_w = mm_w + (tok.get(("exp", 0)), tok.get(("exp", 1)))
            t_mm = E("tensor", mmfn, waits=mm_w, inc="T")
            tok[("yp", par)] = t_mm
            if which == "v":
                dst = bass.AP(Vb, 1040 * tt, [[NTT * 1040, 128],
                                              [vstride, nheads], [1, hd]])
                src = ps[:].rearrange("p (h d) -> p h d", h=nheads)
                t_e = E("vector", lambda e: e.tensor_copy(dst, src),
                        waits=(t_mm, t_ones_v), inc="V", dep=False)
                tok[("psA", par)] = t_e
                tok[("Vb_w", tt)] = t_e
                return
            if is_a:
                t_e = E("vector", lambda e, yb=yb, ps=ps: e.tensor_copy(
                    yb[:], ps[:]), waits=(t_mm, tok.get("ybk")), inc="V",
                    dep=False)
            else:
                t_e = rope_evac(tt, t_mm, dst=yb, ps=ps)
            tok[("psA", par)] = t_e
            tok["ybk"] = t_e
            t_tp2 = transpose8(yb[:], waits=(t_e,), par=par)
            tok["ybk"] = t_tp2
            if which == "k":
                dst = bass.AP(kp, 128 * tt, [[NCH * SL, 128], [SL, NCH], [1, 128]])
                t_o = evac_psT(dst, waits=(t_tp2,), par=par)
                tok[("kp_w", tt)] = t_o
            else:
                i4 = tt % 4
                dst = bass.AP(qp, 128 * i4, [[NCH * 512, 128], [512, NCH], [1, 128]])
                t_o = evac_psT(dst, waits=(t_tp2, tok.get("qp_r")), par=par)
                tok["qp_w"] = t_o

        t_wk = load_w("bk") if not is_a else None
        if KDBG == 57 and not is_a and not dumped[0]:
            dumped[0] = True
            _tc = E("vector", lambda e: e.tensor_copy(scrA[:, 0:512], wbuf[:, 0, 0:512]),
                    waits=(t_wk, tok.get("scrA")), inc="V")
            _tc2 = E("vector", lambda e: e.tensor_copy(scrB[:], wbuf[:, 0, 512:1024]),
                     waits=(tok.get("scrB"),), inc="V")
            dumpf(scrA[:, 0:512], 0, 512, waits=(_tc,))
            _td = dumpf(scrB[:], 512, 512, waits=(_tc2,))
            tok["scrA"] = _td
            tok["scrB"] = _td
        for tt in range(NTT):
            make_tile(tt, "k", t_wk)
            if KDBG == kdb + 1 and tt == 0:
                dump(yb_t[:], D, waits=(tok.get("yb_t"),))
        if KDBG == kdb + 2:
            dump(kp[:].rearrange("p a b -> p (a b)"), NCH * SL,
                 waits=(tok.get(("kp_w", NTT - 1)),))
        if KDBG == kdb + 8 and not dumped[0]:
            dumped[0] = True
            _tc = E("vector", lambda e: e.tensor_copy(scrA[:, 0:512], kp[:, 0, 0:512]),
                    waits=(tok.get(("kp_w", NTT - 1)), tok.get("scrA")), inc="V")
            _tc2 = E("vector", lambda e: e.tensor_copy(scrB[:], kp[:, 0, 512:1024]),
                     waits=(tok.get("scrB"),), inc="V")
            dumpf(scrA[:, 0:512], 0, 512, waits=(_tc,))
            _td = dumpf(scrB[:], 512, 512, waits=(_tc2,))
            tok["scrA"] = _td
            tok["scrB"] = _td
        t_wv = load_w("bv", waits=(tok.get(("yp", 0)), tok.get(("yp", 1)))) if not is_a else None
        for tt in range(NTT):
            make_tile(tt, "v", t_wv)
        if KDBG == kdb + 3:
            dump(Vb[:].rearrange("p a b -> p (a b)"), NTT * 1040,
                 waits=(tok.get(("Vb_w", NTT - 1)),))
        tok["wbuf_r"] = tok.get(("yp", (NTT - 1) % 2))

        t_wq = load_w("bq", waits=(tok.get(("yp", 0)), tok.get(("yp", 1)))) if not is_a else None

        for g in range(NG):
            for i4 in range(4):
                make_tile(4 * g + i4, "q", t_wq)
            t_qgrp = tok["qp_w"]
            if KDBG == kdb + 4 and g == 0:
                dump(qp[:].rearrange("p a b -> p (a b)"), NCH * 512,
                     waits=(t_qgrp,))

            kts = list(range(4 * g + 4))
            for h in range(nheads):
                if is_a:
                    base_p, chunk = 64 * (h % 2), h // 2
                else:
                    base_p, chunk = 0, h
                pv_sched = {0: [], 1: []}
                for kt in kts:
                    for j in range(max(0, kt - 4 * g), 4):
                        pv_sched[j // 2].append((kt, j))
                pv_first = {b: v[0] for b, v in pv_sched.items() if v}
                pv_last = {b: v[-1] for b, v in pv_sched.items() if v}
                padw = 256 - (hd + 1)

                t_pv = None
                pending_pv = None
                for kt in kts:
                    b = kt % 2
                    j0 = max(0, kt - 4 * g)
                    qoff = 128 * j0
                    ncols = 512 - qoff

                    def mm_s(e, kt=kt, b=b, base_p=base_p, chunk=chunk):
                        lhsT = kp[base_p:base_p + hd, chunk,
                                  128 * kt:128 * kt + 128]
                        rhs = qp[base_p:base_p + hd, chunk, 0:512]
                        return e.matmul(psX[:, 512 * b:512 * b + 512], lhsT,
                                        rhs, start=True, stop=True)
                    t_sc = E("tensor", mm_s,
                             waits=(t_qgrp, tok.get(("kp_w", kt)),
                                    tok.get(("exp", b)), tok.get(("msk", b)),
                                    tok.get(("psA", 1))),
                             inc="T")
                    t_ex = E("scalar", lambda e, b=b, qoff=qoff, ncols=ncols:
                             e.activation(ptile[b][:, qoff:qoff + ncols],
                                          psX[:, 512 * b + qoff:
                                              512 * b + qoff + ncols],
                                          AF.Exp, scale=escale),
                             waits=(t_sc, t_consts, tok.get(("pv", b))),
                             inc="S", dep=False)
                    tok[("exp", b)] = t_ex
                    t_gate = t_ex
                    if kt >= 4 * g:
                        jd = kt - 4 * g
                        t_gate = E("vector", lambda e, b=b, jd=jd: e.tensor_mul(
                            ptile[b][:, 128 * jd:128 * jd + 128],
                            ptile[b][:, 128 * jd:128 * jd + 128], mask01[:]),
                            waits=(t_ex, t_mask), inc="V", dep=False)
                        tok[("msk", b)] = t_gate

                    def mk_pv(e, kt=kt, b=b, j0=j0, h=h):
                        r = None
                        for j in range(j0, 4):
                            bank, slot = j // 2, (j % 2) * 256
                            st = pv_first[bank] == (kt, j)
                            r = e.matmul(
                                psY[:, 512 * bank + slot:
                                    512 * bank + slot + hd + 1],
                                ptile[b][:, 128 * j:128 * j + 128],
                                Vb[:, kt, vstride * h:vstride * h + hd + 1],
                                start=st, stop=False)
                        return r
                    if pending_pv is not None:
                        pfn, pw, pb = pending_pv
                        t_pv = E("tensor", pfn, waits=pw, inc="T")
                        tok[("pv", pb)] = t_pv
                    pending_pv = (mk_pv, (t_gate, tok.get(("Vb_w", kt)),
                                          tok.get("pv_evac")), b)
                if pending_pv is not None:
                    pfn, pw, pb = pending_pv
                    t_pv = E("tensor", pfn, waits=pw, inc="T")
                    tok[("pv", pb)] = t_pv

                def mk_pad(e, padw=padw):
                    r = None
                    for bank in (0, 1):
                        for slot in (hd + 1, 256 + hd + 1):
                            r = e.matmul(psY[:, 512 * bank + slot:
                                              512 * bank + slot + padw],
                                         mask01[:], Vb[:, 0, 0:padw],
                                         start=False, stop=(slot > 256))
                    return r
                t_pv = E("tensor", mk_pad,
                         waits=(t_pv, t_mask, tok.get(("Vb_w", 0))), inc="T")
                E("vector", lambda e: e.reciprocal(
                    rs4[:, 0:2], bass.AP(psY, sumcol, [[D, 128], [256, 2]])),
                    waits=(t_pv,), inc="V", dep=False)
                t_r = E("vector", lambda e: e.reciprocal(
                    rs4[:, 2:4], bass.AP(psY, 512 + sumcol,
                                         [[D, 128], [256, 2]])),
                    inc="V", dep=False)
                t_ev = None
                for j in range(4):
                    bank, slot = j // 2, (j % 2) * 256

                    def ev(e, j=j, bank=bank, slot=slot, h=h):
                        return e.tensor_scalar(
                            aotok[j][:, hd * h:hd * h + hd],
                            psY[:, 512 * bank + slot:512 * bank + slot + hd],
                            rs4[:, j:j + 1], 1.0, ALU.mult, ALU.mult)
                    t_ev = E("vector", ev,
                             waits=(t_r, tok.get(("ao_r", j))), inc="V",
                             dep=False)
                    tok[("ao_w", j)] = t_ev
                tok["pv_evac"] = t_ev

            if KDBG == kdb + 5 and g == 0:
                for j in range(4):
                    if j == 0:
                        dump(aotok[0][:], D, waits=(tok.get(("ao_w", 0)),))
            for qt in range(4):
                t_tp = transpose8(aotok[qt][:], waits=(tok.get(("ao_w", qt)),))
                tok[("ao_r", qt)] = t_tp
                dst = bass.AP(aop, 128 * qt, [[NCH * 512, 128], [512, NCH],
                                              [1, 128]])
                t_ao = evac_psT(dst, waits=(t_tp, tok.get("aop_r")))
                tok[("aop_w", qt)] = t_ao
            for s8 in range(8):
                wb = whalf[s8 % 2]
                srcw = scr[pname].rearrange("(c p) e -> p c e", p=128)
                t_ws = E("sync", lambda e, s8=s8, wb=wb, srcw=srcw: e.dma_start(
                    wb[:], srcw[:, :, 128 * s8:128 * s8 + 128]),
                    waits=(tok.get("scr_" + pname),
                           tok.get(("whalf_r", s8 % 2)), tok.get("wh_prev")),
                    inc="wh", inck=16)
                tok["wh_prev"] = t_ws

                pp = psQ[s8 % 2]

                def mm_p2(e, s8=s8, wb=wb, pp=pp):
                    r = None
                    for qt in range(4):
                        for c in range(NCH):
                            st = (qt == 0) and c == 0
                            sp = (not is_a) and (qt == 3) and (c == NCH - 1)
                            r = e.matmul(pp[:, 128 * qt:128 * qt + 128],
                                         aop[:, c, 128 * qt:128 * qt + 128],
                                         wb[:, c, :], start=st, stop=sp)
                    if is_a:
                        for qt in range(4):
                            r = e.matmul(pp[:, 128 * qt:128 * qt + 128],
                                         ones_ap("p"),
                                         bias_ap("p", 128 * s8, 128),
                                         start=False, stop=(qt == 3))
                    return r
                mmw = (t_ws, t_ones, tok.get(("psA", s8 % 2)),
                       tok.get(("aop_w", 0)), tok.get(("aop_w", 1)),
                       tok.get(("aop_w", 2)), tok.get(("aop_w", 3)))
                if s8 % 2 == 1:
                    mmw = mmw + (tok.get(("exp", 0)), tok.get(("exp", 1)))
                t_mm = E("tensor", mm_p2, waits=mmw, inc="T")
                tok[("whalf_r", s8 % 2)] = t_mm
                tok["aop_r"] = t_mm
                t_res = None
                for qt in range(4):
                    tt = 4 * g + qt
                    t_res = E("vector", lambda e, qt=qt, tt=tt, s8=s8, pp=pp:
                              e.tensor_add(x_sb[:, tt, 128 * s8:128 * s8 + 128],
                                           x_sb[:, tt, 128 * s8:128 * s8 + 128],
                                           pp[:, 128 * qt:128 * qt + 128]),
                              waits=(t_mm,), inc="V", dep=False)
                tok[("psA", s8 % 2)] = t_res
                tok["x_w"] = t_res
            if tail is not None:
                tail(g, tok["x_w"])
            tok["qp_r"] = tok["x_w"]
        tok["Vb_r"] = tok["x_w"]
        tok["wbuf_r"] = tok.get(("yp", 1))
        return tok["x_w"]

    ffn_ncall = [0]

    def ffn(t_x, pre=None, tail=None):
        t_xl = t_x if isinstance(t_x, list) else [t_x] * 4
        t_x = t_xl[-1]
        ffn_ncall[0] += 1
        t_w = load_w("fdqT")
        t_stats = ln_stats(t_xl, pre=pre)
        t_last = None
        for tt in range(NTT):
            t_y = apply_ln(tt, scrA[:], t_stats, extra=(tok.get("scrA"),))
            if KDBG == 61 and ffn_ncall[0] == 4 and tt == 3:
                t_y = dumpf(scrA[:], 0, D, waits=(t_y,)) or t_y
            t_am = E("vector", lambda e: e.tensor_reduce(
                amax[:, 0:1], scrA[:], AX.X, ALU.max, apply_absolute_value=True),
                waits=(t_y,), inc="V", dep=False)
            E("vector", lambda e: e.tensor_scalar_max(
                amax[:, 0:1], amax[:, 0:1], 1e-8), inc="V")
            E("vector", lambda e: e.reciprocal(amax[:, 1:2], amax[:, 0:1]), inc="V")
            t_xs = E("vector", lambda e: e.tensor_scalar(
                xsc[:, 0:1], amax[:, 0:1], 1.0 / 127.0, None, ALU.mult),
                inc="V", dep=False)
            E("vector", lambda e: e.tensor_scalar(
                amax[:, 2:3], amax[:, 1:2], 127.0, None, ALU.mult),
                inc="V", dep=False)
            E("vector", lambda e: e.tensor_scalar(
                amax[:, 3:4], amax[:, 1:2], 127.0, None, ALU.mult),
                inc="V", dep=False)
            sl0, sl1 = slice(0, 512), slice(512, 1024)
            pt0, pt1 = ptile[0], ptile[1]
            E("vector", lambda e: e.tensor_scalar(
                scrI[:], scrA[:, sl0], amax[:, 2:3], 1.0, ALU.mult, ALU.mult),
                inc="V", dep=False)
            E("vector", lambda e: e.tensor_scalar(
                scrI1[:], scrA[:, sl1], amax[:, 3:4], 1.0, ALU.mult, ALU.mult),
                inc="V", dep=False)
            E("vector", lambda e: e.scalar_tensor_tensor(
                scrB[:], scrA[:, sl0], amax[:, 2:3], scrI[:], ALU.mult,
                ALU.subtract), waits=(tok.get("scrB"),), inc="V", dep=False)
            E("vector", lambda e: e.scalar_tensor_tensor(
                scrD[:], scrA[:, sl1], amax[:, 3:4], scrI1[:], ALU.mult,
                ALU.subtract), inc="V", dep=False)
            E("vector", lambda e: e.tensor_scalar(
                scrBF[:], scrB[:], 0.5, 1.0, ALU.is_ge, ALU.mult),
                inc="V", dep=False)
            E("vector", lambda e: e.tensor_scalar(
                pt0[:], scrD[:], 0.5, 1.0, ALU.is_ge, ALU.mult),
                inc="V", dep=False)
            E("vector", lambda e: e.tensor_scalar(
                scrC[:], scrB[:], -0.5, 1.0, ALU.is_le, ALU.mult),
                waits=(tok.get("scrC"),), inc="V", dep=False)
            E("vector", lambda e: e.tensor_scalar(
                pt1[:], scrD[:], -0.5, 1.0, ALU.is_le, ALU.mult),
                inc="V", dep=False)
            for thr in (1.5, 2.5):
                E("vector", lambda e, thr=thr: e.scalar_tensor_tensor(
                    scrBF[:], scrB[:], thr, scrBF[:], ALU.is_ge, ALU.add),
                    inc="V", dep=False)
                E("vector", lambda e, thr=thr: e.scalar_tensor_tensor(
                    pt0[:], scrD[:], thr, pt0[:], ALU.is_ge, ALU.add),
                    inc="V", dep=False)
                E("vector", lambda e, thr=thr: e.scalar_tensor_tensor(
                    scrC[:], scrB[:], -thr, scrC[:], ALU.is_le, ALU.add),
                    inc="V", dep=False)
                E("vector", lambda e, thr=thr: e.scalar_tensor_tensor(
                    pt1[:], scrD[:], -thr, pt1[:], ALU.is_le, ALU.add),
                    inc="V", dep=False)
            E("vector", lambda e: e.scalar_tensor_tensor(
                scrB[:], scrC[:], -1.0, scrBF[:], ALU.mult, ALU.add),
                inc="V", dep=False)
            E("vector", lambda e: e.scalar_tensor_tensor(
                scrD[:], pt1[:], -1.0, pt0[:], ALU.mult, ALU.add),
                inc="V", dep=False)
            E("vector", lambda e, tt=tt: e.scalar_tensor_tensor(
                yb_t2[tt % 2][:, sl0], scrB[:], 1.0, scrI[:], ALU.mult,
                ALU.add), waits=(tok.get("ybk"),), inc="V", dep=False)
            tq = E("vector", lambda e, tt=tt: e.scalar_tensor_tensor(
                yb_t2[tt % 2][:, sl1], scrD[:], 1.0, scrI1[:], ALU.mult,
                ALU.add), inc="V", dep=False)
            tok["scrC"] = tq
            tok["scrI"] = tq
            tok["scrB"] = tq
            if KDBG == 61 and ffn_ncall[0] == 4 and tt == 3:
                tq = dump(yb_t[:], D, waits=(tq,)) or tq
                _tx = E("vector", lambda e: e.tensor_copy(scrB[:], bass.AP(
                    xsc, 0, [[1, 128], [0, 512]])), waits=(tq,), inc="V")
                tq = dumpf(scrB[:], 1024, 512, waits=(_tx,)) or tq
            tok["scrA"] = tq
            par = tt % 2
            yb = yb_t2[par]
            yp = yp_t2[par]
            t_tp = transpose8(yb[:], waits=(tq, tok.get("ybk")), par=par)
            tok["ybk"] = t_tp
            t_yp = evac_psT(yp[:].rearrange("p a b -> p (a b)"),
                            waits=(t_tp, tok.get(("yp", par))), par=par)

            ps = psQ[par]

            def mm(e, yp=yp, ps=ps):
                r = None
                for h2 in range(2):
                    for c in range(NCH):
                        r = e.matmul(ps[:, 512 * h2:512 * h2 + 512],
                                     yp[:, c, :],
                                     wbuf[:, c, 512 * h2:512 * h2 + 512],
                                     start=(c == 0), stop=(c == NCH - 1))
                return r
            t_mm = E("tensor", mm,
                     waits=(t_yp, t_w, tok.get(("psA", par))), inc="T")
            tok[("yp", par)] = t_mm
            t_d1 = E("vector", lambda e, ps=ps: e.scalar_tensor_tensor(
                scrA[:], ps[:], xsc[:, 0:1], fbB[:], ALU.mult, ALU.add),
                waits=(t_mm, t_xs, tok.get("scrA")), inc="V", dep=False)
            tok[("psA", par)] = t_d1
            t_last = E("gpsimd", lambda e, tt=tt: e.tensor_add(
                x_sb[:, tt, :], x_sb[:, tt, :], scrA[:]), waits=(t_d1,),
                inc="G")
            tok["scrA"] = t_last
            if tail is not None and tt % 4 == 3:
                tail(tt // 4, t_last)
        tok["wbuf_r"] = tok.get(("yp", (NTT - 1) % 2))
        tok["x_w"] = t_last
        return t_last

    def emit_wcasts():
        tok["pre_prev"] = tok.get("scr_fdqT")
        for name, whsrc in (("aPt", aPt_h), ("bq", bw_h["q"]),
                            ("bk", bw_h["k"]), ("bv", bw_h["v"]),
                            ("bo", bw_h["o"])):
            for c in range(NCH):
                t = dma_in(scrA[:], whsrc[128 * c:128 * c + 128, :],
                           waits=(tok.get("preA"),))
                tg2 = E("gpsimd", lambda e, c=c: e.tensor_copy(
                    wbuf[:, c, :], scrA[:]),
                    waits=(t, tok.get("wbuf_pre")), inc="G")
                tok["preA"] = tg2
            tp = dma_pre(scr[name].rearrange("(c p) e -> p c e", p=128),
                         wbuf[:], waits=(tok.get("pre_prev"),))
            tok["pre_prev"] = tp
            tok["wbuf_pre"] = tp
            tok["scr_" + name] = tp

    gen_tables()
    t_x = x_in(0, chunks=(2, 3))
    emit_wcasts()
    for seq in range(NSEQ):
        t_x5 = t_x
        pre_tok = [None]

        def mk_tail():
            def tail(g, t_tok):
                pre_tok[0] = stats_passes4(g, t_tok, scrB[:], scrC[:])
            return tail
        if KDBG >= 1:
            t_x5 = attn("a", t_x, tail=mk_tail() if KDBG >= 2 else None)
            sch.barrier()
        if KDBG >= 2:
            p = pre_tok[0]
            pre_tok[0] = None
            t_x5 = ffn(t_x5, pre=p, tail=mk_tail() if KDBG >= 3 else None)
            sch.barrier()
        if KDBG >= 3:
            p = pre_tok[0]
            pre_tok[0] = None
            t_x5 = attn("b", t_x5, pre=p,
                        tail=mk_tail() if KDBG >= 4 else None)
            sch.barrier()
        if KDBG >= 4:
            p = pre_tok[0]
            pre_tok[0] = None
            t_x5 = ffn(t_x5, pre=p)
            sch.barrier()
        if isinstance(t_x5, list):
            t_x5 = t_x5[-1]
        dstv = out_hbm[seq].rearrange("(tt p) c -> p tt c", p=128)
        t_out = None
        for ch in range(4):
            t_out = E("gpsimd", lambda e, d=dstv, ch=ch: e.dma_start(
                d[:, 4 * ch:4 * ch + 4, :], x_sb[:, 4 * ch:4 * ch + 4, :]),
                waits=(t_x5, t_out), inc="out", inck=16)
            if seq + 1 < NSEQ:
                t = E("sync", lambda e, ch=ch, seq=seq: e.dma_start(
                    x_sb[:, 4 * ch:4 * ch + 4, :],
                    x_hbm[seq + 1].rearrange(
                        "(tt p) c -> p tt c",
                        p=128)[:, 4 * ch:4 * ch + 4, :]),
                    waits=(t_out, tok.get("xi_prev")), inc="xi", inck=16)
                tok["xi_prev"] = t
                tok[("xch", seq + 1, ch)] = t
        if seq + 1 < NSEQ:
            t_x = [tok[("xch", seq + 1, c)] for c in range(4)]
    E("sync", lambda e: None, waits=((("out", sch.cnt["out"]),)))
    sch.prog["sync"].append(([("out", sch.cnt["out"])], None, None, 0, True))

    with nc.Block() as block:
        sch.replay(block)
    ctx.close()
    return nc


_NC_CACHE = None


def kernel(L0_attn_a_x, L0_attn_a_Wq_w, L0_attn_a_Wq_b, L0_attn_a_Wk_w,
           L0_attn_a_Wk_b, L0_attn_a_Wv_w, L0_attn_a_Wv_b, L0_attn_a_proj_w,
           L0_attn_a_proj_b, L0_attn_b_wq, L0_attn_b_wk, L0_attn_b_wv,
           L0_attn_b_wo, L0_ffn_weight, L0_ffn_bias):
    global _NC_CACHE
    x = np.ascontiguousarray(np.asarray(L0_attn_a_x, dtype=np.float32))

    def bd(w):
        r = np.zeros((128, 128), dtype=np.float32)
        wt = np.ascontiguousarray(np.asarray(w, dtype=np.float32).T)
        r[0:64, 0:64] = wt
        r[64:128, 64:128] = wt
        return r

    ins_common = {
        "aR_q": bd(L0_attn_a_Wq_w),
        "aR_k": bd(L0_attn_a_Wk_w),
        "aR_v": bd(L0_attn_a_Wv_w),
        "ab_q": np.tile(np.asarray(L0_attn_a_Wq_b, np.float32), 16)[None, :],
        "ab_k": np.tile(np.asarray(L0_attn_a_Wk_b, np.float32), 16)[None, :],
        "ab_v": np.tile(np.asarray(L0_attn_a_Wv_b, np.float32), 16)[None, :],
        "ab_p": np.asarray(L0_attn_a_proj_b, np.float32)[None, :],
        "aPt": np.ascontiguousarray(np.asarray(L0_attn_a_proj_w, np.float32).T),
        "bw_q": np.ascontiguousarray(
            np.asarray(L0_attn_b_wq, np.float32).reshape(D, D)),
        "bw_k": np.ascontiguousarray(
            np.asarray(L0_attn_b_wk, np.float32).reshape(D, D)),
        "bw_v": np.ascontiguousarray(
            np.asarray(L0_attn_b_wv, np.float32).reshape(D, D)),
        "bw_o": np.ascontiguousarray(
            np.asarray(L0_attn_b_wo, np.float32).reshape(D, D)),
        "fw": np.ascontiguousarray(np.asarray(L0_ffn_weight, np.float32)),
        "fb": np.asarray(L0_ffn_bias, np.float32)[None, :],
    }
    in_maps = []
    for c in range(NCORES):
        m = dict(ins_common)
        m["x"] = np.ascontiguousarray(x[2 * c:2 * c + 2])
        in_maps.append(m)

    if _NC_CACHE is None:
        _NC_CACHE = build_nc()
    nc = _NC_CACHE
    res = run_bass_kernel_spmd(nc, in_maps, core_ids=list(range(NCORES)))
    out = np.concatenate([r["out"] for r in res.results], axis=0)
    return out.astype(np.float32)


if __name__ == "__main__":
    nc = build_nc()
    total = 0
    print("built ok")


# version 29 (final)
# speedup vs baseline: 1.0153x; 1.0153x over previous
import os
import numpy as np
import ml_dtypes
from contextlib import ExitStack

KDBG = int(os.environ.get("KDBG", "99"))

import concourse.bass as bass
import concourse.mybir as mybir
from concourse.bass_utils import run_bass_kernel_spmd

F32 = mybir.dt.float32
BF16 = mybir.dt.bfloat16
I32 = mybir.dt.int32
ALU = mybir.AluOpType
AF = mybir.ActivationFunctionType
AX = mybir.AxisListType

NCORES = 8
NSEQ = 2
NTT = 16
NCH = 8
SL = 2048
D = 1024
NG = 4

ENGS = ("sync", "gpsimd", "tensor", "vector", "scalar")
PI = float(np.pi)
LN1E4 = float(np.log(10000.0))


class Sched:
    def __init__(self, nc, ctx):
        self.nc = nc
        self.prog = {e: [] for e in ENGS}
        self.sem = {}
        self.cnt = {}
        self.names = ("in", "out", "pre", "wh", "xi", "T", "V", "S", "G")
        self.waited = {(e, s): 0 for e in ENGS for s in self.names}
        for s in self.names:
            self.sem[s] = ctx.enter_context(nc.semaphore("sem_" + s))
            self.cnt[s] = 0

    def emit(self, eng, fn, waits=(), inc=None, inck=1, dep=True):
        wl = []
        for w in waits:
            if w is None:
                continue
            s, v = w
            if v > self.waited[(eng, s)]:
                self.waited[(eng, s)] = v
                wl.append((s, v))
        tokn = None
        if inc is not None:
            self.cnt[inc] += inck
            tokn = (inc, self.cnt[inc])
        self.prog[eng].append((wl, fn, inc if inc is not None else None,
                               inck if inc is not None else 0, dep))
        return tokn

    def barrier(self):
        marks = [(s, self.cnt[s])
                 for s in ("T", "V", "S", "G", "in", "pre", "wh", "xi")
                 if self.cnt[s] > 0]
        for e in ENGS:
            wl = []
            for s, v in marks:
                if v > self.waited[(e, s)]:
                    self.waited[(e, s)] = v
                    wl.append((s, v))
            if wl:
                self.prog[e].append((wl, None, None, 0, True))

    def replay(self, block):
        sem = self.sem

        def mk(eng_name):
            needs_drain = eng_name in ("vector", "scalar")

            def body(engine):
                prev_compute = False
                for wl, fn, inc, inck, dep in self.prog[eng_name]:
                    for s, v in wl:
                        engine.wait_ge(sem[s], v)
                    if fn is not None:
                        if needs_drain and prev_compute and dep:
                            engine.drain()
                        r = fn(engine)
                        prev_compute = True
                        if inc is not None:
                            r.then_inc(sem[inc], inck)
            return body
        block.sync(mk("sync"))
        block.gpsimd(mk("gpsimd"))
        block.tensor(mk("tensor"))
        block.vector(mk("vector"))
        block.scalar(mk("scalar"))


def build_nc():
    nc = bass.Bass("TRN2", target_bir_lowering=False, detect_race_conditions=False)
    ctx = ExitStack()

    x_hbm = nc.dram_tensor("x", [NSEQ, SL, D], F32, kind="ExternalInput")
    out_hbm = nc.dram_tensor("out", [NSEQ, SL, D], F32, kind="ExternalOutput")
    aR_h = {w: nc.dram_tensor(f"aR_{w}", [128, 128], F32, kind="ExternalInput")
            for w in ("q", "k", "v")}
    ab_h = {w: nc.dram_tensor(f"ab_{w}", [1, D], F32, kind="ExternalInput")
            for w in ("q", "k", "v", "p")}
    aPt_h = nc.dram_tensor("aPt", [D, D], F32, kind="ExternalInput")
    bw_h = {w: nc.dram_tensor(f"bw_{w}", [D, D], F32, kind="ExternalInput")
            for w in ("q", "k", "v", "o")}
    fw_h = nc.dram_tensor("fw", [D, D], F32, kind="ExternalInput")
    fb_h = nc.dram_tensor("fb", [1, D], F32, kind="ExternalInput")

    ident_h = nc.inline_tensor(np.eye(128, dtype=ml_dtypes.bfloat16), "identity")
    mnp = np.zeros((128, 128), dtype=ml_dtypes.bfloat16)
    for i in range(128):
        mnp[i, i:] = 1.0
    mask_h = nc.inline_tensor(mnp, "mask01")

    scr = {w: nc.dram_tensor(f"scr_{w}", [D, D], BF16, kind="Internal")
           for w in ("aPt", "bq", "bk", "bv", "bo", "fdqT")}
    dump_h = (nc.dram_tensor("dump", [128, 20480], BF16, kind="ExternalOutput")
              if KDBG < 90 else None)
    dumpf_h = (nc.dram_tensor("dumpf", [128, 2048], F32, kind="ExternalOutput")
               if KDBG < 90 else None)

    sb = lambda n, shp, dt=F32: ctx.enter_context(nc.sbuf_tensor(n, shp, dt))
    x_sb = sb("x_sb", [128, NTT, D])
    ident = sb("ident_sb", [128, 128], BF16)
    mask01 = sb("mask01_sb", [128, 128], BF16)
    consts = sb("consts", [128, 10])
    ones1 = sb("ones1", [65, 128], BF16)
    sums = sb("sums", [128, NTT])
    sumsq = sb("sumsq", [128, NTT])
    lnm = sb("lnm", [128, NTT])
    lnv = sb("lnv", [128, NTT])
    lnr = sb("lnr", [128, NTT])
    negmr = sb("negmr", [128, NTT])
    amax = sb("amax", [128, 4])
    xsc = sb("xsc", [128, 1])
    rs4 = sb("rs4", [128, 4])
    scrA = sb("scrA", [128, D])
    scrB = sb("scrB", [128, 512])
    scrC = sb("scrC", [128, 512])
    scrI = ctx.enter_context(nc.sbuf_tensor("scrI", [128, 512], I32))
    scrI1 = ctx.enter_context(nc.sbuf_tensor("scrI1", [128, 512], I32))
    scrBF = sb("scrBF", [128, 512], BF16)
    _yb0 = sb("yb_t0", [128, D], BF16)
    yb_t2 = [_yb0, _yb0]
    yp_t2 = [sb("yp_t0", [128, NCH, 128], BF16), sb("yp_t1", [128, NCH, 128], BF16)]
    yb_t = yb_t2[0]
    yp_t = yp_t2[0]
    kp = sb("kp", [128, NCH, SL], BF16)
    Vb = sb("Vb", [128, NTT, 1040], BF16)
    qp = sb("qp", [128, NCH, 512], BF16)
    aop = sb("aop", [128, NCH, 512], BF16)
    aotok = [sb(f"aotok{j}", [128, D], BF16) for j in range(4)]
    ptile = [sb(f"pt{j}", [128, 512], BF16) for j in range(2)]
    aRb = {w: sb(f"aRb_{w}", [128, 128], BF16) for w in ("q", "k", "v")}
    wbuf = ctx.enter_context(nc.sbuf_tensor("wbuf", [128, NCH, D], BF16))
    whalf = [ctx.enter_context(nc.sbuf_tensor("whalf0", [128, NCH, 128], BF16)),
             ctx.enter_context(nc.sbuf_tensor("whalf1", [128, NCH, 128], BF16))]
    tabc = sb("tabc", [128, NTT, 32], BF16)
    tabs = sb("tabs", [128, NTT, 64], BF16)
    biasr3 = sb("biasr3", [65, D], BF16)
    biasrp = sb("biasrp", [1, D], BF16)
    biasrow = {"q": 0, "k": 32, "v": 64}
    def bias_ap(w, c0, cn):
        if w == "p":
            return biasrp[0:1, c0:c0 + cn]
        r = biasrow[w]
        return biasr3[r:r + 1, c0:c0 + cn]
    fbB = sb("fbB", [128, D], BF16)

    psc = lambda n, shp, dt=F32: ctx.enter_context(nc.psum_tensor(n, shp, dt))
    psA = psc("psA", [128, D])
    psX = psc("psX", [128, D])
    psY = psc("psY", [128, D])
    psQ = [psA, psX]
    psT2 = [psc("psT0", [128, D], BF16), psc("psT1", [128, D], BF16)]
    psT = psT2[0]

    sch = Sched(nc, ctx)
    E = sch.emit
    tok = {}

    dumped = [False]

    def dump(src_ap, ncols, waits=()):
        if dumped[0] or dump_h is None:
            return None
        dumped[0] = True
        t = E("gpsimd", lambda e: e.dma_start(dump_h.ap()[:, 0:ncols], src_ap),
              waits=waits, inc="out", inck=16)
        return t

    def dumpf(src_ap, col0, ncols, waits=()):
        if dumpf_h is None:
            return None
        t = E("gpsimd", lambda e: e.dma_start(
            dumpf_h.ap()[:, col0:col0 + ncols], src_ap),
            waits=waits, inc="out", inck=16)
        return t

    def dma_in(dst_ap, src_ap, waits=()):
        return E("sync", lambda e, s=src_ap, d=dst_ap: e.dma_start(d, s),
                 waits=waits, inc="in", inck=16)

    def dma_pre(dst_ap, src_ap, waits=()):
        return E("gpsimd", lambda e, s=src_ap, d=dst_ap: e.dma_start(d, s),
                 waits=waits, inc="pre", inck=16)

    _early_tbl = True
    t_ident = dma_in(ident[:], ident_h.ap())
    t_mask = dma_in(mask01[:], mask_h.ap())
    E("vector", lambda e: e.memset(consts[:, 0:1], 0.125), inc="V")
    E("vector", lambda e: e.memset(consts[:, 1:2], 1.0 / 128.0), inc="V")
    E("vector", lambda e: e.memset(consts[:, 2:3], 1e-5), inc="V")
    t_consts = E("vector", lambda e: e.memset(consts[:, 3:4], -0.5), inc="V")
    t_ones = E("vector", lambda e: e.memset(ones1[:], 1.0), inc="V")
    def ones_ap(w):
        r = 0 if w == "p" else biasrow[w]
        return ones1[r:r + 1, :]

    load_table0 = None
    for w in ("q", "k", "v"):
        t = dma_in(scrA[:, 0:128], aR_h[w].ap(), waits=(tok.get("scrA"),))
        tv = E("vector", lambda e, w=w: e.tensor_copy(aRb[w][:], scrA[:, 0:128]),
               waits=(t,), inc="V")
        tok["scrA"] = tv
    for w in ("q", "k", "v", "p"):
        t = dma_in(scrA[0:1, :], ab_h[w].ap(), waits=(tok.get("scrA"),))
        tv = E("vector", lambda e, w=w: e.tensor_copy(
            bias_ap(w, 0, D), scrA[0:1, :]),
               waits=(t,), inc="V")
        tok["scrA"] = tv
    t = dma_in(scrA[:], bass.AP(fb_h, 0, [[0, 128], [1, D]]),
               waits=(tok.get("scrA"),))
    tg = E("vector", lambda e: e.tensor_copy(fbB[:], scrA[:]),
           waits=(t,), inc="V")
    tok["scrA"] = tg


    t_fw = dma_in(x_sb[:, 8:16, :],
                  fw_h.rearrange("(c p) e -> p c e", p=128))
    for ch in range(2):
        t = E("sync", lambda e, ch=ch: e.dma_start(
            x_sb[:, 4 * ch:4 * ch + 4, :],
            x_hbm[0].rearrange("(tt p) c -> p tt c",
                               p=128)[:, 4 * ch:4 * ch + 4, :]),
            waits=(tok.get("xi_prev"),), inc="xi", inck=16)
        tok["xi_prev"] = t
        tok[("xch", 0, ch)] = t
    for c in range(NCH):
        t1 = E("vector", lambda e, c=c: e.tensor_reduce(
            amax[:, 0:1], x_sb[:, 8 + c, :], AX.X, ALU.max,
            apply_absolute_value=True),
            waits=(t_fw,), inc="V")
        E("vector", lambda e: e.tensor_scalar_max(amax[:, 0:1], amax[:, 0:1], 1e-8),
          inc="V")
        t5a = E("vector", lambda e: e.tensor_scalar(
            xsc[:, 0:1], amax[:, 0:1], 1.0 / 127.0, None, ALU.mult), inc="V")
        E("vector", lambda e: e.reciprocal(amax[:, 1:2], xsc[:, 0:1]), inc="V")
        t5 = E("vector", lambda e: e.tensor_scalar(
            rs4[:, 0:1], amax[:, 1:2], -1.0, None, ALU.mult), inc="V")
        for hf in range(2):
            sl_ = slice(512 * hf, 512 * hf + 512)
            ybs = yb_t[:, sl_]
            V = lambda fn, w=(): E("vector", fn, waits=w, inc="V")
            V(lambda e, sl_=sl_, ybs=ybs, c=c: e.tensor_scalar(
                scrC[:], x_sb[:, 8 + c, sl_], amax[:, 1:2], 1.0, ALU.mult,
                ALU.mult))
            V(lambda e, sl_=sl_, ybs=ybs, c=c: e.scalar_tensor_tensor(
                scrD[:], scrC[:], xsc[:, 0:1], x_sb[:, 8 + c, sl_], ALU.mult,
                ALU.subtract))
            V(lambda e, sl_=sl_, ybs=ybs: e.scalar_tensor_tensor(
                scrC[:], scrD[:], rs4[:, 0:1], scrC[:], ALU.mult, ALU.add))
            V(lambda e, sl_=sl_, ybs=ybs: e.tensor_scalar(scrI[:], scrC[:], 1.0, None, ALU.mult))
            V(lambda e, sl_=sl_, ybs=ybs: e.scalar_tensor_tensor(
                scrB[:], scrC[:], 1.0, scrI[:], ALU.mult, ALU.subtract))
            V(lambda e, sl_=sl_, ybs=ybs: e.tensor_scalar(scrD[:], scrI[:], 1.0, None, ALU.mult))
            V(lambda e, sl_=sl_, ybs=ybs: e.tensor_scalar(scrC[:], scrD[:], 0.5, -0.25,
                                        ALU.mult, ALU.add))
            V(lambda e, sl_=sl_, ybs=ybs: e.tensor_scalar(scrI[:], scrC[:], 1.0, None, ALU.mult))
            V(lambda e, sl_=sl_, ybs=ybs: e.scalar_tensor_tensor(scrC[:], scrC[:], 1.0, scrI[:],
                                               ALU.mult, ALU.subtract))
            V(lambda e, sl_=sl_, ybs=ybs: e.tensor_scalar(scrBF[:], scrC[:], 0.5, 1.0,
                                        ALU.is_ge, ALU.mult))
            V(lambda e, sl_=sl_, ybs=ybs: e.tensor_scalar(scrC[:], scrC[:], -0.5, 1.0,
                                        ALU.is_le, ALU.mult))
            V(lambda e, sl_=sl_, ybs=ybs: e.scalar_tensor_tensor(scrC[:], scrC[:], -1.0, scrBF[:],
                                               ALU.mult, ALU.add))
            V(lambda e, sl_=sl_, ybs=ybs: e.scalar_tensor_tensor(scrC[:], scrC[:], 1.0, scrI[:],
                                               ALU.mult, ALU.add))
            V(lambda e, sl_=sl_, ybs=ybs: e.scalar_tensor_tensor(scrC[:], scrC[:], -2.0, scrD[:],
                                               ALU.mult, ALU.add))
            V(lambda e, sl_=sl_, ybs=ybs: e.tensor_scalar(scrBF[:], scrB[:], 0.5, 1.0,
                                        ALU.is_gt, ALU.mult))
            V(lambda e, sl_=sl_, ybs=ybs: e.tensor_scalar(ybs, scrB[:], -0.5, 1.0,
                                        ALU.is_lt, ALU.mult),
              w=(tok.get("yb_t"),))
            V(lambda e, sl_=sl_, ybs=ybs: e.scalar_tensor_tensor(scrBF[:], ybs, -1.0, scrBF[:],
                                               ALU.mult, ALU.add))
            V(lambda e, sl_=sl_, ybs=ybs: e.tensor_scalar(ybs, scrB[:], 0.5, 1.0,
                                        ALU.is_equal, ALU.mult))
            V(lambda e, sl_=sl_, ybs=ybs: e.tensor_scalar(scrB[:], scrB[:], -0.5, 1.0,
                                        ALU.is_equal, ALU.mult))
            V(lambda e, sl_=sl_, ybs=ybs: e.scalar_tensor_tensor(scrB[:], scrB[:], -1.0, ybs,
                                               ALU.mult, ALU.add))
            V(lambda e, sl_=sl_, ybs=ybs: e.tensor_mul(scrB[:], scrB[:], scrC[:]))
            V(lambda e, sl_=sl_, ybs=ybs: e.scalar_tensor_tensor(scrB[:], scrB[:], 1.0, scrBF[:],
                                               ALU.mult, ALU.add))
            V(lambda e, sl_=sl_, ybs=ybs: e.scalar_tensor_tensor(scrB[:], scrB[:], 1.0, scrD[:],
                                               ALU.mult, ALU.add))
            tq = E("vector", lambda e, ybs=ybs: e.tensor_scalar(
                ybs, scrB[:], xsc[:, 0:1], 1.0, ALU.mult, ALU.mult),
                waits=(tok.get("yb_t"),), inc="V")
        tok["scrA"] = tq
        ttr = E("tensor", lambda e: [
            e.transpose(psT[:, 128 * b:128 * b + 128],
                        yb_t[:, 128 * b:128 * b + 128], ident[:])
            for b in range(NCH)][-1],
            waits=(tq, t_ident, tok.get("psT")), inc="T")
        te = E("scalar", lambda e: e.copy(scrA[:].bitcast(BF16)[:, 0:1024], psT[:])
               if False else e.copy(yp_t[:].rearrange("p a b -> p (a b)"), psT[:]),
               waits=(ttr, tok.get("yp_t")), inc="S")
        tok["psT"] = te
        tpl = None
        for b in range(NCH):
            tpl = dma_pre(scr["fdqT"][128 * b:128 * b + 128, 128 * c:128 * c + 128],
                          yp_t[:, b, :], waits=(te,))
        tok["yp_t"] = tpl
        tok["yb_t"] = ttr
        tok["scr_fdqT"] = tpl

    sch.barrier()

    def load_w(name, waits=()):
        src = scr[name].rearrange("(c p) e -> p c e", p=128)
        return E("sync", lambda e, s=src: e.dma_start(wbuf[:], s),
                 waits=tuple(w for w in waits) + (tok.get("scr_" + name),
                                                  tok.get("wbuf_r"),
                                                  tok.get("wbuf_pre")),
                 inc="in", inck=16)

    def x_in(seq, waits=(), chunks=(0, 1, 2, 3)):
        src = x_hbm[seq].rearrange("(tt p) c -> p tt c", p=128)
        ts = []
        for ch in range(4):
            if ch not in chunks:
                ts.append(tok[("xch", seq, ch)])
                continue
            t = E("sync", lambda e, s=src, ch=ch: e.dma_start(
                x_sb[:, 4 * ch:4 * ch + 4, :], s[:, 4 * ch:4 * ch + 4, :]),
                waits=tuple(waits) + (tok.get("xi_prev"),), inc="xi", inck=16)
            tok["xi_prev"] = t
            tok[("xch", seq, ch)] = t
            ts.append(t)
        return ts

    def stats_passes4(g, t_tok, junkB, junkC):
        last = None
        for tt in range(4 * g, 4 * g + 4):
            last = E("vector", lambda e, tt=tt: e.tensor_scalar(
                x_sb[:, tt, :], x_sb[:, tt, :], 1.0, 0.0, ALU.mult, ALU.add,
                accum_out=sums[:, tt:tt + 1]),
                waits=(t_tok,), inc="V", dep=False)
            last = E("vector", lambda e, tt=tt: e.scalar_tensor_tensor(
                junkB, x_sb[:, tt, 0:512], 1.0, x_sb[:, tt, 0:512], ALU.mult,
                ALU.mult, accum_out=sumsq[:, tt:tt + 1]),
                inc="V", dep=False)
            last = E("vector", lambda e, tt=tt: e.scalar_tensor_tensor(
                junkC, x_sb[:, tt, 512:1024], 1.0, x_sb[:, tt, 512:1024],
                ALU.mult, ALU.mult, accum_out=lnv[:, tt:tt + 1]),
                inc="V", dep=False)
        return last

    def ln_stats(t_x, pre=None):
        txl = t_x if isinstance(t_x, list) else [t_x] * 4
        if pre is not None:
            lv = ls = pre
        else:
            lv = ls = None
            for tt in range(NTT):
                tx = txl[tt // 4]
                lv = E("vector", lambda e, tt=tt: e.tensor_scalar(
                    x_sb[:, tt, :], x_sb[:, tt, :], 1.0, 0.0, ALU.mult,
                    ALU.add, accum_out=sums[:, tt:tt + 1]),
                    waits=(tx,), inc="V", dep=(tt == 0))
                ls = E("vector", lambda e, tt=tt: e.scalar_tensor_tensor(
                    psY[:, 0:512], x_sb[:, tt, 0:512], 1.0,
                    x_sb[:, tt, 0:512], ALU.mult, ALU.mult,
                    accum_out=sumsq[:, tt:tt + 1]),
                    inc="V", dep=False)
                ls = E("vector", lambda e, tt=tt: e.scalar_tensor_tensor(
                    psY[:, 512:1024], x_sb[:, tt, 512:1024], 1.0,
                    x_sb[:, tt, 512:1024], ALU.mult, ALU.mult,
                    accum_out=lnv[:, tt:tt + 1]),
                    inc="V", dep=False)
        t_dbg = None
        if KDBG == 31 and not dumped[0]:
            dumped[0] = True
            dumpf(sumsq[:], 0, NTT, waits=(lv, ls))
            dumpf(lnv[:], 16, NTT, waits=(lv, ls))
            t_dbg = dumpf(sums[:], 32, NTT, waits=(lv, ls))
        E("vector", lambda e: e.tensor_scalar(
            lnm[:], sums[:], 1.0 / D, None, ALU.mult),
            waits=(lv, ls, t_dbg), inc="V", dep=False)
        ta1 = E("vector", lambda e: e.tensor_add(lnv[:], lnv[:], sumsq[:]),
                inc="V", dep=False)
        E("vector", lambda e: e.scalar_tensor_tensor(
            sumsq[:], lnm[:], -1.0, lnm[:], ALU.mult, ALU.mult), inc="V",
            dep=False)
        E("vector", lambda e: e.tensor_scalar(
            lnv[:], lnv[:], 1.0 / D, None, ALU.mult),
            waits=(ta1,), inc="V", dep=False)
        t1 = E("vector", lambda e: e.tensor_add(lnv[:], lnv[:], sumsq[:]),
               inc="V")
        t2 = E("scalar", lambda e: e.activation(
            lnr[:], lnv[:], AF.Ln, bias=consts[:, 2:3]),
            waits=(t1, t_consts), inc="S")
        t3e = E("scalar", lambda e: e.activation(
            lnr[:], lnr[:], AF.Exp, scale=consts[:, 3:4]), inc="S")
        t4 = E("vector", lambda e: e.scalar_tensor_tensor(
            negmr[:], lnm[:], -1.0, lnr[:], ALU.mult, ALU.mult),
            waits=(t3e,), inc="V")
        if KDBG == 28 and not dumped[0]:
            dumped[0] = True
            dumpf(sums[:], 0, NTT, waits=(t4,))
            dumpf(lnm[:], 16, NTT, waits=(t4,))
            dumpf(lnv[:], 32, NTT, waits=(t4,))
            dumpf(lnr[:], 48, NTT, waits=(t4,))
            dumpf(sumsq[:], 64, NTT, waits=(t4,))
        return t4

    def apply_ln(tt, dst_ap, t_stats, extra=()):
        return E("scalar", lambda e: e.activation(
            dst_ap, x_sb[:, tt, :], AF.Identity, scale=lnr[:, tt:tt + 1],
            bias=negmr[:, tt:tt + 1]), waits=(t_stats,) + tuple(extra),
            inc="S", dep=(tt == 0))

    def transpose8(src_tile, waits=(), par=0):
        ps = psT2[par]
        def fn(e, ps=ps):
            r = None
            for b in range(NCH):
                r = e.transpose(ps[:, 128 * b:128 * b + 128],
                                src_tile[:, 128 * b:128 * b + 128], ident[:])
            return r
        return E("tensor", fn,
                 waits=tuple(waits) + (tok.get(("psT", par)), t_ident), inc="T")

    def evac_psT(dst_ap, waits=(), par=0):
        ps = psT2[par]
        t = E("scalar", lambda e, ps=ps: e.copy(dst_ap, ps[:]),
              waits=waits, inc="S", dep=False)
        tok[("psT", par)] = t
        return t

    def rope_evac(tt, t_mm, dst=None, ps=None):
        yb_t = dst
        fs = ps.shape[1]
        rot = bass.AP(ps, 0, [[fs, 128], [128, 8], [1, 64]])
        swp = bass.AP(ps, 32, [[fs, 128], [128, 8], [-32, 2], [1, 32]])
        skp = bass.AP(ps, 64, [[fs, 128], [128, 8], [1, 64]])
        c2 = bass.AP(tabc, 32 * tt, [[NTT * 32, 128], [0, 8], [0, 2], [1, 32]])
        s2 = bass.AP(tabs, 64 * tt, [[NTT * 64, 128], [0, 8], [1, 64]])
        ds_ = yb_t.shape[1]
        d_rot = bass.AP(yb_t, 0, [[ds_, 128], [128, 8], [1, 64]])
        d_skp = bass.AP(yb_t, 64, [[ds_, 128], [128, 8], [1, 64]])
        sA = scrB[:].rearrange("p (a b) -> p a b", a=8)
        sB = scrC[:].rearrange("p (a b) -> p a b", a=8)
        E("vector", lambda e: e.tensor_tensor(sA, swp, s2, ALU.mult),
          waits=(t_mm, tok.get("tabs_ok"), tok.get("scrB")),
          inc="V", dep=False)
        E("vector", lambda e: e.tensor_tensor(sB, rot, c2, ALU.mult),
          waits=(tok.get("scrC"),), inc="V", dep=False)
        E("vector", lambda e: e.tensor_copy(d_skp, skp), inc="V", dep=False)
        t3 = E("vector", lambda e: e.tensor_tensor(d_rot, sB, sA, ALU.add),
               inc="V", dep=False)
        tok["scrB"] = t3
        tok["scrC"] = t3
        return t3

    def gen_tables():
        tc0 = E("vector", lambda e: e.memset(consts[:, 9:10], 10000.0), inc="V")
        tc1 = E("scalar", lambda e: e.activation(consts[:, 8:9], consts[:, 9:10],
                                                 AF.Ln), waits=(tc0,), inc="S")
        tc2 = E("vector", lambda e: e.tensor_scalar(
            consts[:, 8:9], consts[:, 8:9], -1.0 / 32.0, None, ALU.mult),
            waits=(tc1,), inc="V")
        tc3 = E("vector", lambda e: e.memset(consts[:, 9:10], 1.0),
                waits=(tc1,), inc="V")
        load_table(_set_trig)
        tc4 = E("scalar", lambda e: e.activation(consts[:, 4:5], consts[:, 9:10],
                                                 AF.Arctan), waits=(tc3,), inc="S")
        tc5 = E("vector", lambda e: e.tensor_scalar(
            consts[:, 4:5], consts[:, 4:5], 4.0, None, ALU.mult),
            waits=(tc4,), inc="V")
        E("vector", lambda e: e.tensor_scalar(
            consts[:, 5:6], consts[:, 4:5], -1.0, None, ALU.mult), inc="V")
        E("vector", lambda e: e.reciprocal(consts[:, 6:7], consts[:, 4:5]), inc="V")
        tc6 = E("vector", lambda e: e.tensor_scalar(
            consts[:, 7:8], consts[:, 4:5], 0.5, None, ALU.mult), inc="V")
        tg1 = E("gpsimd", lambda e: e.iota(scrB[:, 0:NTT], [[128, NTT]], base=0,
                                           channel_multiplier=1,
                                           allow_small_or_imprecise_dtypes=True),
                waits=(tok.get("scrB"),), inc="G")
        tg2 = E("gpsimd", lambda e: e.iota(scrB[:, 16:48], [[1, 32]], base=0,
                                           channel_multiplier=0,
                                           allow_small_or_imprecise_dtypes=True),
                inc="G")
        t2 = E("scalar", lambda e: e.activation(scrB[:, 48:80], scrB[:, 16:48],
                                                AF.Exp, scale=consts[:, 8:9]),
               waits=(tg2, tc2, tc6), inc="S")
        fsB = scrB.shape[1]

        def mk_rad(e):
            src_pos = bass.AP(scrB, 0, [[fsB, 128], [1, NTT], [0, 64]])
            src_ang = bass.AP(scrB, 48, [[fsB, 128], [0, NTT], [0, 2], [1, 32]])
            return e.tensor_tensor(
                psY[:].rearrange("p (a b) -> p a b", a=NTT), src_pos, src_ang,
                ALU.mult)
        t3 = E("vector", mk_rad, waits=(t2, tg1), inc="V")
        if KDBG == 49 and not dumped[0]:
            dumped[0] = True
            t3 = dumpf(psY[:], 0, D, waits=(t3,)) or t3
        if KDBG == 50 and not dumped[0]:
            dumped[0] = True
            t3 = dumpf(scrB[:, 0:80], 0, 80, waits=(t3,)) or t3

        def pipeline(dst_tab, shift, tprev):
            last = tprev
            if shift:
                srcs = [(bass.AP(psY, 0, [[D, 128], [64, NTT], [1, 32]]),
                         dst_tab[:].rearrange("p a b -> p (a b)"))]
            else:
                full = dst_tab[:].rearrange("p a b -> p (a b)")
                srcs = [(psY[:, 0:512], full[:, 0:512]),
                        (psY[:, 512:1024], full[:, 512:1024])]
            for src_ap, dst_ap in srcs:
                t0 = E("vector", lambda e, src_ap=src_ap, shift=shift:
                       e.tensor_scalar(
                           scrC[:].rearrange("p (a b) -> p a b", a=NTT)
                           if shift else scrC[:],
                           src_ap, 1.0,
                           consts[:, 7:8] if shift else 0.0, ALU.mult, ALU.add),
                    waits=(last, tok.get("scrC")), inc="V")
                E("vector", lambda e: e.tensor_scalar(
                    scrI[:], scrC[:], consts[:, 6:7], 1.0, ALU.mult, ALU.mult),
                    inc="V")
                E("vector", lambda e: e.scalar_tensor_tensor(
                    scrB[:], scrC[:], consts[:, 6:7], scrI[:], ALU.mult,
                    ALU.subtract), waits=(tok.get("scrB"),), inc="V")
                E("vector", lambda e: e.tensor_scalar(
                    scrBF[:], scrB[:], 0.5, 1.0, ALU.is_ge, ALU.mult), inc="V")
                E("vector", lambda e: e.tensor_scalar(
                    scrB[:], scrB[:], -0.5, 1.0, ALU.is_le, ALU.mult), inc="V")
                E("vector", lambda e: e.scalar_tensor_tensor(
                    scrB[:], scrB[:], -1.0, scrBF[:], ALU.mult, ALU.add), inc="V")
                E("vector", lambda e: e.scalar_tensor_tensor(
                    scrB[:], scrB[:], 1.0, scrI[:], ALU.mult, ALU.add), inc="V")
                t4 = E("vector", lambda e: e.scalar_tensor_tensor(
                    scrC[:], scrB[:], consts[:, 5:6], scrC[:], ALU.mult,
                    ALU.add), inc="V")
                E("vector", lambda e: e.tensor_scalar(
                    scrI[:], scrB[:], 0.5, -0.25, ALU.mult, ALU.add), inc="V")
                E("vector", lambda e: e.tensor_scalar(
                    scrD[:], scrB[:], 0.5, -0.25, ALU.mult, ALU.add), inc="V")
                E("vector", lambda e: e.scalar_tensor_tensor(
                    scrD[:], scrD[:], 1.0, scrI[:], ALU.mult, ALU.subtract), inc="V")
                E("vector", lambda e: e.tensor_scalar(
                    scrBF[:], scrD[:], 0.5, 1.0, ALU.is_ge, ALU.mult), inc="V")
                E("vector", lambda e: e.tensor_scalar(
                    scrD[:], scrD[:], -0.5, 1.0, ALU.is_le, ALU.mult), inc="V")
                E("vector", lambda e: e.scalar_tensor_tensor(
                    scrD[:], scrD[:], -1.0, scrBF[:], ALU.mult, ALU.add), inc="V")
                E("vector", lambda e: e.scalar_tensor_tensor(
                    scrD[:], scrD[:], 1.0, scrI[:], ALU.mult, ALU.add), inc="V")
                E("vector", lambda e: e.scalar_tensor_tensor(
                    scrB[:], scrD[:], -2.0, scrB[:], ALU.mult, ALU.add), inc="V")
                t5 = E("vector", lambda e: e.tensor_scalar(
                    scrB[:], scrB[:], -2.0, 1.0, ALU.mult, ALU.add), inc="V")
                if KDBG == 51 and hf == 0 and not dumped[0]:
                    dumped[0] = True
                    dumpf(scrC[:], 0, 512, waits=(t4,))
                    t5 = dumpf(scrB[:], 512, 512, waits=(t5,)) or t5
                t6 = E("scalar", lambda e: e.activation(scrD[:], scrC[:], AF.Sin),
                       waits=(t4, t5), inc="S")
                if KDBG == 52 and hf == 0 and not dumped[0]:
                    dumped[0] = True
                    t6 = dumpf(scrD[:], 0, 512, waits=(t6,)) or t6
                    t5 = dumpf(scrB[:], 512, 512, waits=(t5,)) or t5
                last = E("vector", lambda e, dst_ap=dst_ap: e.tensor_tensor(
                    dst_ap, scrB[:], scrD[:], ALU.mult),
                    waits=(t5, t6), inc="V")
                if KDBG == 54 and hf == 0 and not dumped[0]:
                    dumped[0] = True
                    last = E("vector", lambda e: e.tensor_copy(
                        scrA[:, 0:512],
                        dst_tab[:].rearrange("p a b -> p (a b)")[:, 0:512]),
                        waits=(last,), inc="V")
                    last = dumpf(scrA[:, 0:512], 0, 512, waits=(last,)) or last
            tok["scrB"] = last
            tok["scrC"] = last
            return last

        tc = pipeline(tabc, True, t3)
        if KDBG == 53 and not dumped[0]:
            dumped[0] = True
            dump(tabc[:].rearrange("p a b -> p (a b)"), NTT * 64, waits=(tc,))
        tsn = pipeline(tabs, False, tc)
        tf = E("vector", lambda e: e.tensor_scalar(
            bass.AP(tabs, 0, [[NTT * 64, 128], [64, NTT], [1, 32]]),
            bass.AP(tabs, 0, [[NTT * 64, 128], [64, NTT], [1, 32]]),
            -1.0, None, ALU.mult), waits=(tsn,), inc="V")
        load_table(_set_lnexp)
        tok["tabs_ok"] = tf
        return tf

    scrD = sb("scrD", [128, 512])

    from concourse.hw_specs import get_activation_tables
    _tables = list(get_activation_tables(nc.m.arch).values())
    _set_lnexp = next(i for i, fs in enumerate(_tables)
                      if AF.Exp in fs and AF.Ln in fs)
    _set_trig = next(i for i, fs in enumerate(_tables) if AF.Sin in fs)

    def load_table(set_id):
        def fn(e):
            return e.add_instruction(mybir.InstLoadActFuncSet(
                name=nc.get_next_instruction_name(), act_func_set_id=set_id,
                ins=[], outs=[]))
        E("scalar", fn, inc=None)

    load_table(_set_lnexp)

    def attn(kind, t_x, pre=None, tail=None, t_wk=None):
        t_xl = t_x if isinstance(t_x, list) else [t_x] * 4
        t_x = t_xl[-1]
        is_a = kind == "a"
        nheads = 16 if is_a else 8
        hd = 64 if is_a else 128
        vstride = 65 if is_a else 130
        sumcol = hd
        escale = consts[:, 0:1] if is_a else consts[:, 1:2]
        kdb = 20 if is_a else 40
        pname = "aPt" if is_a else "bo"

        if KDBG == 59 and not is_a and not dumped[0]:
            dumped[0] = True
            _td = dumpf(x_sb[:, 0, 0:512], 0, 512, waits=(t_x,))
            _td = dumpf(x_sb[:, 0, 512:1024], 512, 512, waits=(t_x,))
        if not is_a:
            if KDBG == 46 and not dumped[0]:
                dumped[0] = True
                dump(tabc[:].rearrange("p a b -> p (a b)"), NTT * 64,
                     waits=(tok.get("tabs_ok"),))
            if KDBG == 48 and not dumped[0]:
                dumped[0] = True
                dump(tabs[:].rearrange("p a b -> p (a b)"), NTT * 64,
                     waits=(tok.get("tabs_ok"),))

        t_stats = ln_stats(t_xl, pre=pre)
        if KDBG == 60 and not is_a and not dumped[0]:
            dumped[0] = True
            dumpf(lnr[:], 0, NTT, waits=(t_stats,))
            dumpf(negmr[:], 16, NTT, waits=(t_stats,))
            dumpf(lnm[:], 32, NTT, waits=(t_stats,))
            dumpf(lnv[:], 48, NTT, waits=(t_stats,))
            _td = dumpf(x_sb[:, 0, 0:512], 512, 512, waits=(t_stats,))
            tok["x_dump"] = _td

        t_ones_v = E("vector", lambda e: e.memset(
            bass.AP(Vb, sumcol, [[NTT * 1040, 128], [1040, NTT],
                                 [vstride, nheads]]), 1.0),
            waits=(tok.get("Vb_r"),), inc="V", dep=False)

        def qkv_mm_a(which, yp, ps):
            def fn(e):
                r = None
                for c in range(NCH):
                    r = e.matmul(ps[:, 128 * c:128 * c + 128],
                                 yp[:, c, :], aRb[which][:],
                                 start=c in (0, 4), stop=False)
                for h2 in range(2):
                    r = e.matmul(ps[:, 512 * h2:512 * h2 + 512],
                                 ones_ap(which),
                                 bias_ap(which, 512 * h2, 512),
                                 start=False, stop=True)
                return r
            return fn

        def qkv_mm_b(yp, ps):
            def fn(e):
                r = None
                for h2 in range(2):
                    for c in range(NCH):
                        r = e.matmul(ps[:, 512 * h2:512 * h2 + 512], yp[:, c, :],
                                     wbuf[:, c, 512 * h2:512 * h2 + 512],
                                     start=(c == 0), stop=(c == NCH - 1))
                return r
            return fn

        def make_tile(tt, which, t_w):
            par = tt % 2
            yb = yb_t2[par]
            yp = yp_t2[par]
            ps = psQ[par]
            t_y = apply_ln(tt, yb[:], t_stats, extra=(tok.get("ybk"),))
            t_tp = transpose8(yb[:], waits=(t_y,), par=par)
            tok["ybk"] = t_tp
            t_yp = evac_psT(yp[:].rearrange("p a b -> p (a b)"),
                            waits=(t_tp, tok.get(("yp", par))), par=par)
            mmfn = (qkv_mm_a(which, yp, ps) if is_a
                    else qkv_mm_b(yp, ps))
            mm_w = (t_yp, t_w, t_ones, tok.get(("psA", par)))
            if par == 1:
                mm_w = mm_w + (tok.get(("exp", 0)), tok.get(("exp", 1)))
            t_mm = E("tensor", mmfn, waits=mm_w, inc="T")
            tok[("yp", par)] = t_mm
            if which == "v":
                dst = bass.AP(Vb, 1040 * tt, [[NTT * 1040, 128],
                                              [vstride, nheads], [1, hd]])
                src = ps[:].rearrange("p (h d) -> p h d", h=nheads)
                t_e = E("vector", lambda e: e.tensor_copy(dst, src),
                        waits=(t_mm, t_ones_v), inc="V", dep=False)
                tok[("psA", par)] = t_e
                tok[("Vb_w", tt)] = t_e
                return
            if is_a:
                t_e = E("vector", lambda e, yb=yb, ps=ps: e.tensor_copy(
                    yb[:], ps[:]), waits=(t_mm, tok.get("ybk")), inc="V",
                    dep=False)
            else:
                t_e = rope_evac(tt, t_mm, dst=yb, ps=ps)
            tok[("psA", par)] = t_e
            tok["ybk"] = t_e
            t_tp2 = transpose8(yb[:], waits=(t_e,), par=par)
            tok["ybk"] = t_tp2
            if which == "k":
                dst = bass.AP(kp, 128 * tt, [[NCH * SL, 128], [SL, NCH], [1, 128]])
                t_o = evac_psT(dst, waits=(t_tp2,), par=par)
                tok[("kp_w", tt)] = t_o
            else:
                i4 = tt % 4
                dst = bass.AP(qp, 128 * i4, [[NCH * 512, 128], [512, NCH], [1, 128]])
                t_o = evac_psT(dst, waits=(t_tp2, tok.get("qp_r")), par=par)
                tok["qp_w"] = t_o

        if t_wk is None:
            t_wk = load_w("bk") if not is_a else None
        if KDBG == 57 and not is_a and not dumped[0]:
            dumped[0] = True
            _tc = E("vector", lambda e: e.tensor_copy(scrA[:, 0:512], wbuf[:, 0, 0:512]),
                    waits=(t_wk, tok.get("scrA")), inc="V")
            _tc2 = E("vector", lambda e: e.tensor_copy(scrB[:], wbuf[:, 0, 512:1024]),
                     waits=(tok.get("scrB"),), inc="V")
            dumpf(scrA[:, 0:512], 0, 512, waits=(_tc,))
            _td = dumpf(scrB[:], 512, 512, waits=(_tc2,))
            tok["scrA"] = _td
            tok["scrB"] = _td
        for tt in range(NTT):
            make_tile(tt, "k", t_wk)
            if KDBG == kdb + 1 and tt == 0:
                dump(yb_t[:], D, waits=(tok.get("yb_t"),))
        if KDBG == kdb + 2:
            dump(kp[:].rearrange("p a b -> p (a b)"), NCH * SL,
                 waits=(tok.get(("kp_w", NTT - 1)),))
        if KDBG == kdb + 8 and not dumped[0]:
            dumped[0] = True
            _tc = E("vector", lambda e: e.tensor_copy(scrA[:, 0:512], kp[:, 0, 0:512]),
                    waits=(tok.get(("kp_w", NTT - 1)), tok.get("scrA")), inc="V")
            _tc2 = E("vector", lambda e: e.tensor_copy(scrB[:], kp[:, 0, 512:1024]),
                     waits=(tok.get("scrB"),), inc="V")
            dumpf(scrA[:, 0:512], 0, 512, waits=(_tc,))
            _td = dumpf(scrB[:], 512, 512, waits=(_tc2,))
            tok["scrA"] = _td
            tok["scrB"] = _td
        t_wv = load_w("bv", waits=(tok.get(("yp", 0)), tok.get(("yp", 1)))) if not is_a else None
        for tt in range(NTT):
            make_tile(tt, "v", t_wv)
        if KDBG == kdb + 3:
            dump(Vb[:].rearrange("p a b -> p (a b)"), NTT * 1040,
                 waits=(tok.get(("Vb_w", NTT - 1)),))
        tok["wbuf_r"] = tok.get(("yp", (NTT - 1) % 2))

        t_wq = load_w("bq", waits=(tok.get(("yp", 0)), tok.get(("yp", 1)))) if not is_a else None

        for g in range(NG):
            for i4 in range(4):
                make_tile(4 * g + i4, "q", t_wq)
            t_qgrp = tok["qp_w"]
            if KDBG == kdb + 4 and g == 0:
                dump(qp[:].rearrange("p a b -> p (a b)"), NCH * 512,
                     waits=(t_qgrp,))

            kts = list(range(4 * g + 4))
            for h in range(nheads):
                if is_a:
                    base_p, chunk = 64 * (h % 2), h // 2
                else:
                    base_p, chunk = 0, h
                pv_sched = {0: [], 1: []}
                for kt in kts:
                    for j in range(max(0, kt - 4 * g), 4):
                        pv_sched[j // 2].append((kt, j))
                pv_first = {b: v[0] for b, v in pv_sched.items() if v}
                pv_last = {b: v[-1] for b, v in pv_sched.items() if v}
                padw = 256 - (hd + 1)

                t_pv = None
                pending_pv = None
                for kt in kts:
                    b = kt % 2
                    j0 = max(0, kt - 4 * g)
                    qoff = 128 * j0
                    ncols = 512 - qoff

                    def mm_s(e, kt=kt, b=b, base_p=base_p, chunk=chunk):
                        lhsT = kp[base_p:base_p + hd, chunk,
                                  128 * kt:128 * kt + 128]
                        rhs = qp[base_p:base_p + hd, chunk, 0:512]
                        return e.matmul(psX[:, 512 * b:512 * b + 512], lhsT,
                                        rhs, start=True, stop=True)
                    t_sc = E("tensor", mm_s,
                             waits=(t_qgrp, tok.get(("kp_w", kt)),
                                    tok.get(("exp", b)), tok.get(("msk", b)),
                                    tok.get(("psA", 1))),
                             inc="T")
                    t_ex = E("scalar", lambda e, b=b, qoff=qoff, ncols=ncols:
                             e.activation(ptile[b][:, qoff:qoff + ncols],
                                          psX[:, 512 * b + qoff:
                                              512 * b + qoff + ncols],
                                          AF.Exp, scale=escale),
                             waits=(t_sc, t_consts, tok.get(("pv", b))),
                             inc="S", dep=False)
                    tok[("exp", b)] = t_ex
                    t_gate = t_ex
                    if kt >= 4 * g:
                        jd = kt - 4 * g
                        t_gate = E("vector", lambda e, b=b, jd=jd: e.tensor_mul(
                            ptile[b][:, 128 * jd:128 * jd + 128],
                            ptile[b][:, 128 * jd:128 * jd + 128], mask01[:]),
                            waits=(t_ex, t_mask), inc="V", dep=False)
                        tok[("msk", b)] = t_gate

                    def mk_pv(e, kt=kt, b=b, j0=j0, h=h):
                        r = None
                        for j in range(j0, 4):
                            bank, slot = j // 2, (j % 2) * 256
                            st = pv_first[bank] == (kt, j)
                            r = e.matmul(
                                psY[:, 512 * bank + slot:
                                    512 * bank + slot + hd + 1],
                                ptile[b][:, 128 * j:128 * j + 128],
                                Vb[:, kt, vstride * h:vstride * h + hd + 1],
                                start=st, stop=False)
                        return r
                    if pending_pv is not None:
                        pfn, pw, pb = pending_pv
                        t_pv = E("tensor", pfn, waits=pw, inc="T")
                        tok[("pv", pb)] = t_pv
                    pending_pv = (mk_pv, (t_gate, tok.get(("Vb_w", kt)),
                                          tok.get("pv_evac")), b)
                if pending_pv is not None:
                    pfn, pw, pb = pending_pv
                    t_pv = E("tensor", pfn, waits=pw, inc="T")
                    tok[("pv", pb)] = t_pv

                def mk_pad(e, padw=padw):
                    r = None
                    for bank in (0, 1):
                        for slot in (hd + 1, 256 + hd + 1):
                            r = e.matmul(psY[:, 512 * bank + slot:
                                              512 * bank + slot + padw],
                                         mask01[:], Vb[:, 0, 0:padw],
                                         start=False, stop=(slot > 256))
                    return r
                t_pv = E("tensor", mk_pad,
                         waits=(t_pv, t_mask, tok.get(("Vb_w", 0))), inc="T")
                E("vector", lambda e: e.reciprocal(
                    rs4[:, 0:2], bass.AP(psY, sumcol, [[D, 128], [256, 2]])),
                    waits=(t_pv,), inc="V", dep=False)
                t_r = E("vector", lambda e: e.reciprocal(
                    rs4[:, 2:4], bass.AP(psY, 512 + sumcol,
                                         [[D, 128], [256, 2]])),
                    inc="V", dep=False)
                t_ev = None
                for j in range(4):
                    bank, slot = j // 2, (j % 2) * 256

                    def ev(e, j=j, bank=bank, slot=slot, h=h):
                        return e.tensor_scalar(
                            aotok[j][:, hd * h:hd * h + hd],
                            psY[:, 512 * bank + slot:512 * bank + slot + hd],
                            rs4[:, j:j + 1], 1.0, ALU.mult, ALU.mult)
                    t_ev = E("vector", ev,
                             waits=(t_r, tok.get(("ao_r", j))), inc="V",
                             dep=False)
                    tok[("ao_w", j)] = t_ev
                tok["pv_evac"] = t_ev

            if KDBG == kdb + 5 and g == 0:
                for j in range(4):
                    if j == 0:
                        dump(aotok[0][:], D, waits=(tok.get(("ao_w", 0)),))
            for qt in range(4):
                t_tp = transpose8(aotok[qt][:], waits=(tok.get(("ao_w", qt)),))
                tok[("ao_r", qt)] = t_tp
                dst = bass.AP(aop, 128 * qt, [[NCH * 512, 128], [512, NCH],
                                              [1, 128]])
                t_ao = evac_psT(dst, waits=(t_tp, tok.get("aop_r")))
                tok[("aop_w", qt)] = t_ao
            for s8 in range(8):
                wb = whalf[s8 % 2]
                srcw = scr[pname].rearrange("(c p) e -> p c e", p=128)
                t_ws = E("sync", lambda e, s8=s8, wb=wb, srcw=srcw: e.dma_start(
                    wb[:], srcw[:, :, 128 * s8:128 * s8 + 128]),
                    waits=(tok.get("scr_" + pname),
                           tok.get(("whalf_r", s8 % 2)), tok.get("wh_prev")),
                    inc="wh", inck=16)
                tok["wh_prev"] = t_ws

                pp = psQ[s8 % 2]

                def mm_p2(e, s8=s8, wb=wb, pp=pp):
                    r = None
                    for qt in range(4):
                        for c in range(NCH):
                            st = (qt == 0) and c == 0
                            sp = (not is_a) and (qt == 3) and (c == NCH - 1)
                            r = e.matmul(pp[:, 128 * qt:128 * qt + 128],
                                         aop[:, c, 128 * qt:128 * qt + 128],
                                         wb[:, c, :], start=st, stop=sp)
                    if is_a:
                        for qt in range(4):
                            r = e.matmul(pp[:, 128 * qt:128 * qt + 128],
                                         ones_ap("p"),
                                         bias_ap("p", 128 * s8, 128),
                                         start=False, stop=(qt == 3))
                    return r
                mmw = (t_ws, t_ones, tok.get(("psA", s8 % 2)),
                       tok.get(("aop_w", 0)), tok.get(("aop_w", 1)),
                       tok.get(("aop_w", 2)), tok.get(("aop_w", 3)))
                if s8 % 2 == 1:
                    mmw = mmw + (tok.get(("exp", 0)), tok.get(("exp", 1)))
                t_mm = E("tensor", mm_p2, waits=mmw, inc="T")
                tok[("whalf_r", s8 % 2)] = t_mm
                tok["aop_r"] = t_mm
                t_res = None
                for qt in range(4):
                    tt = 4 * g + qt
                    t_res = E("vector", lambda e, qt=qt, tt=tt, s8=s8, pp=pp:
                              e.tensor_add(x_sb[:, tt, 128 * s8:128 * s8 + 128],
                                           x_sb[:, tt, 128 * s8:128 * s8 + 128],
                                           pp[:, 128 * qt:128 * qt + 128]),
                              waits=(t_mm,), inc="V", dep=False)
                tok[("psA", s8 % 2)] = t_res
                tok["x_w"] = t_res
            if tail is not None:
                tail(g, tok["x_w"])
            tok["qp_r"] = tok["x_w"]
        tok["Vb_r"] = tok["x_w"]
        tok["wbuf_r"] = tok.get(("yp", 1))
        return tok["x_w"]

    ffn_ncall = [0]

    def ffn(t_x, pre=None, tail=None, t_w=None):
        t_xl = t_x if isinstance(t_x, list) else [t_x] * 4
        t_x = t_xl[-1]
        ffn_ncall[0] += 1
        if t_w is None:
            t_w = load_w("fdqT")
        t_stats = ln_stats(t_xl, pre=pre)
        t_last = None
        for tt in range(NTT):
            t_y = apply_ln(tt, scrA[:], t_stats, extra=(tok.get("scrA"),))
            if KDBG == 61 and ffn_ncall[0] == 4 and tt == 3:
                t_y = dumpf(scrA[:], 0, D, waits=(t_y,)) or t_y
            t_am = E("vector", lambda e: e.tensor_reduce(
                amax[:, 0:1], scrA[:], AX.X, ALU.max, apply_absolute_value=True),
                waits=(t_y,), inc="V", dep=False)
            E("vector", lambda e: e.tensor_scalar_max(
                amax[:, 0:1], amax[:, 0:1], 1e-8), inc="V")
            E("vector", lambda e: e.reciprocal(amax[:, 1:2], amax[:, 0:1]), inc="V")
            t_xs = E("vector", lambda e: e.tensor_scalar(
                xsc[:, 0:1], amax[:, 0:1], 1.0 / 127.0, None, ALU.mult),
                inc="V", dep=False)
            E("vector", lambda e: e.tensor_scalar(
                amax[:, 2:3], amax[:, 1:2], 127.0, None, ALU.mult),
                inc="V", dep=False)
            E("vector", lambda e: e.tensor_scalar(
                amax[:, 3:4], amax[:, 1:2], 127.0, None, ALU.mult),
                inc="V", dep=False)
            sl0, sl1 = slice(0, 512), slice(512, 1024)
            pt0, pt1 = ptile[0], ptile[1]
            E("vector", lambda e: e.tensor_scalar(
                scrI[:], scrA[:, sl0], amax[:, 2:3], 1.0, ALU.mult, ALU.mult),
                inc="V", dep=False)
            E("vector", lambda e: e.tensor_scalar(
                scrI1[:], scrA[:, sl1], amax[:, 3:4], 1.0, ALU.mult, ALU.mult),
                inc="V", dep=False)
            E("vector", lambda e: e.scalar_tensor_tensor(
                scrB[:], scrA[:, sl0], amax[:, 2:3], scrI[:], ALU.mult,
                ALU.subtract), waits=(tok.get("scrB"),), inc="V", dep=False)
            E("vector", lambda e: e.scalar_tensor_tensor(
                scrD[:], scrA[:, sl1], amax[:, 3:4], scrI1[:], ALU.mult,
                ALU.subtract), inc="V", dep=False)
            E("vector", lambda e: e.tensor_scalar(
                scrBF[:], scrB[:], 0.5, 1.0, ALU.is_ge, ALU.mult),
                inc="V", dep=False)
            E("vector", lambda e: e.tensor_scalar(
                pt0[:], scrD[:], 0.5, 1.0, ALU.is_ge, ALU.mult),
                inc="V", dep=False)
            E("vector", lambda e: e.tensor_scalar(
                scrC[:], scrB[:], -0.5, 1.0, ALU.is_le, ALU.mult),
                waits=(tok.get("scrC"),), inc="V", dep=False)
            E("vector", lambda e: e.tensor_scalar(
                pt1[:], scrD[:], -0.5, 1.0, ALU.is_le, ALU.mult),
                inc="V", dep=False)
            for thr in (1.5, 2.5):
                E("vector", lambda e, thr=thr: e.scalar_tensor_tensor(
                    scrBF[:], scrB[:], thr, scrBF[:], ALU.is_ge, ALU.add),
                    inc="V", dep=False)
                E("vector", lambda e, thr=thr: e.scalar_tensor_tensor(
                    pt0[:], scrD[:], thr, pt0[:], ALU.is_ge, ALU.add),
                    inc="V", dep=False)
                E("vector", lambda e, thr=thr: e.scalar_tensor_tensor(
                    scrC[:], scrB[:], -thr, scrC[:], ALU.is_le, ALU.add),
                    inc="V", dep=False)
                E("vector", lambda e, thr=thr: e.scalar_tensor_tensor(
                    pt1[:], scrD[:], -thr, pt1[:], ALU.is_le, ALU.add),
                    inc="V", dep=False)
            E("vector", lambda e: e.scalar_tensor_tensor(
                scrB[:], scrC[:], -1.0, scrBF[:], ALU.mult, ALU.add),
                inc="V", dep=False)
            E("vector", lambda e: e.scalar_tensor_tensor(
                scrD[:], pt1[:], -1.0, pt0[:], ALU.mult, ALU.add),
                inc="V", dep=False)
            E("vector", lambda e, tt=tt: e.scalar_tensor_tensor(
                yb_t2[tt % 2][:, sl0], scrB[:], 1.0, scrI[:], ALU.mult,
                ALU.add), waits=(tok.get("ybk"),), inc="V", dep=False)
            tq = E("vector", lambda e, tt=tt: e.scalar_tensor_tensor(
                yb_t2[tt % 2][:, sl1], scrD[:], 1.0, scrI1[:], ALU.mult,
                ALU.add), inc="V", dep=False)
            tok["scrC"] = tq
            tok["scrI"] = tq
            tok["scrB"] = tq
            if KDBG == 61 and ffn_ncall[0] == 4 and tt == 3:
                tq = dump(yb_t[:], D, waits=(tq,)) or tq
                _tx = E("vector", lambda e: e.tensor_copy(scrB[:], bass.AP(
                    xsc, 0, [[1, 128], [0, 512]])), waits=(tq,), inc="V")
                tq = dumpf(scrB[:], 1024, 512, waits=(_tx,)) or tq
            tok["scrA"] = tq
            par = tt % 2
            yb = yb_t2[par]
            yp = yp_t2[par]
            t_tp = transpose8(yb[:], waits=(tq, tok.get("ybk")), par=par)
            tok["ybk"] = t_tp
            t_yp = evac_psT(yp[:].rearrange("p a b -> p (a b)"),
                            waits=(t_tp, tok.get(("yp", par))), par=par)

            ps = psQ[par]

            def mm(e, yp=yp, ps=ps):
                r = None
                for h2 in range(2):
                    for c in range(NCH):
                        r = e.matmul(ps[:, 512 * h2:512 * h2 + 512],
                                     yp[:, c, :],
                                     wbuf[:, c, 512 * h2:512 * h2 + 512],
                                     start=(c == 0), stop=(c == NCH - 1))
                return r
            t_mm = E("tensor", mm,
                     waits=(t_yp, t_w, tok.get(("psA", par))), inc="T")
            tok[("yp", par)] = t_mm
            t_d1 = E("vector", lambda e, ps=ps: e.scalar_tensor_tensor(
                scrA[:], ps[:], xsc[:, 0:1], fbB[:], ALU.mult, ALU.add),
                waits=(t_mm, t_xs, tok.get("scrA")), inc="V", dep=False)
            tok[("psA", par)] = t_d1
            t_last = E("vector", lambda e, tt=tt: e.tensor_add(
                x_sb[:, tt, :], x_sb[:, tt, :], scrA[:]), waits=(t_d1,), inc="V")
            tok["scrA"] = t_last
            if tail is not None and tt % 4 == 3:
                tail(tt // 4, t_last)
        tok["wbuf_r"] = tok.get(("yp", (NTT - 1) % 2))
        tok["x_w"] = t_last
        return t_last

    def emit_wcasts():
        tok["pre_prev"] = tok.get("scr_fdqT")
        for name, whsrc in (("aPt", aPt_h), ("bq", bw_h["q"]),
                            ("bk", bw_h["k"]), ("bv", bw_h["v"]),
                            ("bo", bw_h["o"])):
            for c in range(NCH):
                t = dma_in(scrA[:], whsrc[128 * c:128 * c + 128, :],
                           waits=(tok.get("preA"),))
                tg2 = E("gpsimd", lambda e, c=c: e.tensor_copy(
                    wbuf[:, c, :], scrA[:]),
                    waits=(t, tok.get("wbuf_pre")), inc="G")
                tok["preA"] = tg2
            tp = dma_pre(scr[name].rearrange("(c p) e -> p c e", p=128),
                         wbuf[:], waits=(tok.get("pre_prev"),))
            tok["pre_prev"] = tp
            tok["wbuf_pre"] = tp
            tok["scr_" + name] = tp

    gen_tables()
    t_x = x_in(0, chunks=(2, 3))
    emit_wcasts()
    for seq in range(NSEQ):
        t_x5 = t_x
        pre_tok = [None]

        def mk_tail():
            def tail(g, t_tok):
                pre_tok[0] = stats_passes4(g, t_tok, scrB[:], scrC[:])
            return tail
        if KDBG >= 1:
            t_x5 = attn("a", t_x, tail=mk_tail() if KDBG >= 2 else None)
            t_wpre = load_w("fdqT") if KDBG >= 2 else None
            sch.barrier()
        if KDBG >= 2:
            p = pre_tok[0]
            pre_tok[0] = None
            t_x5 = ffn(t_x5, pre=p, tail=mk_tail() if KDBG >= 3 else None,
                       t_w=t_wpre)
            t_wpre = load_w("bk") if KDBG >= 3 else None
            sch.barrier()
        if KDBG >= 3:
            p = pre_tok[0]
            pre_tok[0] = None
            t_x5 = attn("b", t_x5, pre=p,
                        tail=mk_tail() if KDBG >= 4 else None, t_wk=t_wpre)
            t_wpre = load_w("fdqT") if KDBG >= 4 else None
            sch.barrier()
        if KDBG >= 4:
            p = pre_tok[0]
            pre_tok[0] = None
            t_x5 = ffn(t_x5, pre=p, t_w=t_wpre)
            sch.barrier()
        if isinstance(t_x5, list):
            t_x5 = t_x5[-1]
        dstv = out_hbm[seq].rearrange("(tt p) c -> p tt c", p=128)
        t_out = None
        for ch in range(4):
            t_out = E("gpsimd", lambda e, d=dstv, ch=ch: e.dma_start(
                d[:, 4 * ch:4 * ch + 4, :], x_sb[:, 4 * ch:4 * ch + 4, :]),
                waits=(t_x5, t_out), inc="out", inck=16)
            if seq + 1 < NSEQ:
                t = E("sync", lambda e, ch=ch, seq=seq: e.dma_start(
                    x_sb[:, 4 * ch:4 * ch + 4, :],
                    x_hbm[seq + 1].rearrange(
                        "(tt p) c -> p tt c",
                        p=128)[:, 4 * ch:4 * ch + 4, :]),
                    waits=(t_out, tok.get("xi_prev")), inc="xi", inck=16)
                tok["xi_prev"] = t
                tok[("xch", seq + 1, ch)] = t
        if seq + 1 < NSEQ:
            t_x = [tok[("xch", seq + 1, c)] for c in range(4)]
    E("sync", lambda e: None, waits=((("out", sch.cnt["out"]),)))
    sch.prog["sync"].append(([("out", sch.cnt["out"])], None, None, 0, True))

    with nc.Block() as block:
        sch.replay(block)
    ctx.close()
    return nc


_NC_CACHE = None


def kernel(L0_attn_a_x, L0_attn_a_Wq_w, L0_attn_a_Wq_b, L0_attn_a_Wk_w,
           L0_attn_a_Wk_b, L0_attn_a_Wv_w, L0_attn_a_Wv_b, L0_attn_a_proj_w,
           L0_attn_a_proj_b, L0_attn_b_wq, L0_attn_b_wk, L0_attn_b_wv,
           L0_attn_b_wo, L0_ffn_weight, L0_ffn_bias):
    global _NC_CACHE
    x = np.ascontiguousarray(np.asarray(L0_attn_a_x, dtype=np.float32))

    def bd(w):
        r = np.zeros((128, 128), dtype=np.float32)
        wt = np.ascontiguousarray(np.asarray(w, dtype=np.float32).T)
        r[0:64, 0:64] = wt
        r[64:128, 64:128] = wt
        return r

    ins_common = {
        "aR_q": bd(L0_attn_a_Wq_w),
        "aR_k": bd(L0_attn_a_Wk_w),
        "aR_v": bd(L0_attn_a_Wv_w),
        "ab_q": np.tile(np.asarray(L0_attn_a_Wq_b, np.float32), 16)[None, :],
        "ab_k": np.tile(np.asarray(L0_attn_a_Wk_b, np.float32), 16)[None, :],
        "ab_v": np.tile(np.asarray(L0_attn_a_Wv_b, np.float32), 16)[None, :],
        "ab_p": np.asarray(L0_attn_a_proj_b, np.float32)[None, :],
        "aPt": np.ascontiguousarray(np.asarray(L0_attn_a_proj_w, np.float32).T),
        "bw_q": np.ascontiguousarray(
            np.asarray(L0_attn_b_wq, np.float32).reshape(D, D)),
        "bw_k": np.ascontiguousarray(
            np.asarray(L0_attn_b_wk, np.float32).reshape(D, D)),
        "bw_v": np.ascontiguousarray(
            np.asarray(L0_attn_b_wv, np.float32).reshape(D, D)),
        "bw_o": np.ascontiguousarray(
            np.asarray(L0_attn_b_wo, np.float32).reshape(D, D)),
        "fw": np.ascontiguousarray(np.asarray(L0_ffn_weight, np.float32)),
        "fb": np.asarray(L0_ffn_bias, np.float32)[None, :],
    }
    in_maps = []
    for c in range(NCORES):
        m = dict(ins_common)
        m["x"] = np.ascontiguousarray(x[2 * c:2 * c + 2])
        in_maps.append(m)

    if _NC_CACHE is None:
        _NC_CACHE = build_nc()
    nc = _NC_CACHE
    res = run_bass_kernel_spmd(nc, in_maps, core_ids=list(range(NCORES)))
    out = np.concatenate([r["out"] for r in res.results], axis=0)
    return out.astype(np.float32)


if __name__ == "__main__":
    nc = build_nc()
    total = 0
    print("built ok")
